# version 05 (final)
# baseline (speedup 1.0000x reference)
import contextlib
import os
import numpy as np
import ml_dtypes
import concourse.bass as bass
import concourse.mybir as mybir
from concourse.bass_utils import run_bass_kernel_spmd

F32 = mybir.dt.float32
BF16 = mybir.dt.bfloat16
AF = mybir.ActivationFunctionType
ALU = mybir.AluOpType
AX = mybir.AxisListType

NCORES = 8
B_LOC = 2
L = 2048
T = B_LOC * L
D = 1024
NB = 512
BF = ml_dtypes.bfloat16
CONST_COLS = {"eps5": (0, 1e-5), "one": (1, 1.0), "eps6": (2, 1e-6)}

STREAMS = ("tensor", "vector", "scalar", "gpsimd", "syncq", "gpq", "scq")
STREAM_ENGINE = {"tensor": "tensor", "vector": "vector", "scalar": "scalar",
                 "gpsimd": "gpsimd", "syncq": "sync", "gpq": "gpsimd",
                 "scq": "scalar"}
STREAM_DMA = {"tensor": False, "vector": False, "scalar": False,
              "gpsimd": False, "syncq": True, "gpq": True, "scq": True}


class _Op:
    __slots__ = ("stream", "fn", "deps", "idx", "milestone", "semval", "waits")

    def __init__(self, stream, fn, idx):
        self.stream = stream
        self.fn = fn
        self.idx = idx
        self.deps = []
        self.milestone = False
        self.semval = 0
        self.waits = []


class Prog:
    def __init__(self, nc):
        self.nc = nc
        self.stack = contextlib.ExitStack()
        self.pstacks = []
        self.ops = {s: [] for s in STREAMS}
        self.order = []
        self.last_w = {}
        self.readers = {}
        self.pend_bar = {s: [] for s in STREAMS}
        self.uid = 0

    def _st(self):
        return self.pstacks[-1] if self.pstacks else self.stack

    def sbuf(self, name, shape, dtype):
        self.uid += 1
        return self._st().enter_context(
            self.nc.sbuf_tensor(f"{name}_{self.uid}", list(shape), dtype))

    def psum(self, name, shape, dtype=F32):
        self.uid += 1
        return self._st().enter_context(
            self.nc.psum_tensor(f"{name}_{self.uid}", list(shape), dtype))

    def sem(self, name):
        return self.stack.enter_context(self.nc.semaphore(name))

    @contextlib.contextmanager
    def phase(self):
        self.pstacks.append(contextlib.ExitStack())
        try:
            yield
        finally:
            self.pstacks.pop().close()
            self.barrier()

    def op(self, stream, fn, reads=(), writes=()):
        ops = self.ops[stream]
        o = _Op(stream, fn, len(ops))
        deps = {}

        def add_dep(st, ix):
            if st == stream and (stream == "tensor" or STREAM_DMA[stream]):
                return
            if st == stream and ix == len(ops) - 1 and False:
                return
            if ix > deps.get(st, -1):
                deps[st] = ix

        for r in reads:
            w = self.last_w.get(r)
            if w is not None:
                add_dep(*w)
        for r in writes:
            w = self.last_w.get(r)
            if w is not None:
                add_dep(*w)
            for rd in self.readers.get(r, ()):
                add_dep(*rd)
        for st, ix in self.pend_bar[stream]:
            add_dep(st, ix)
        self.pend_bar[stream] = []
        o.deps = sorted(deps.items())
        ops.append(o)
        self.order.append((stream, o.idx))
        me = (stream, o.idx)
        for r in reads:
            self.readers.setdefault(r, []).append(me)
        for r in writes:
            self.last_w[r] = me
            self.readers[r] = []
        return o

    def barrier(self):
        lasts = [(s, len(self.ops[s]) - 1) for s in STREAMS if self.ops[s]]
        for s in STREAMS:
            self.pend_bar[s] = [(t, i) for (t, i) in lasts if t != s]

    def dma(self, qstream, out, in_, reads=(), writes=(), **kw):
        return self.op(qstream, lambda e: e.dma_start(out, in_, **kw),
                       reads=reads, writes=writes)

    def finalize(self):
        nc = self.nc
        for s in STREAMS:
            for o in self.ops[s]:
                for (ts, ti) in o.deps:
                    self.ops[ts][ti].milestone = True
        finals = []
        for s in STREAMS:
            if STREAM_DMA[s] and self.ops[s]:
                for o in self.ops[s]:
                    o.milestone = True
                finals.append((s, self.ops[s][-1].idx))
        for s in STREAMS:
            c = 0
            for o in self.ops[s]:
                if o.milestone:
                    c += 16 if STREAM_DMA[s] else 1
                    o.semval = c
        know = {s: {t: 0 for t in STREAMS} for s in STREAMS}
        for (s, ix) in self.order:
            o = self.ops[s][ix]
            for (ts, ti) in o.deps:
                tv = self.ops[ts][ti].semval
                assert tv > 0
                if know[s][ts] < tv:
                    o.waits.append((ts, tv))
                    know[s][ts] = tv
        sems = {s: self.sem("sem_" + s) for s in STREAMS}
        per_engine = {}
        for (s, ix) in self.order:
            per_engine.setdefault(STREAM_ENGINE[s], []).append(self.ops[s][ix])
        final_by_engine = {}
        for (s, ix) in finals:
            final_by_engine.setdefault(STREAM_ENGINE[s], []).append(
                (s, self.ops[s][ix].semval))

        with nc.Block() as block:
            def emit(eng_name, eng):
                for o in per_engine.get(eng_name, ()):
                    for (ts, tv) in o.waits:
                        eng.wait_ge(sems[ts], tv)
                    inst = o.fn(eng)
                    if o.milestone:
                        inst.then_inc(sems[o.stream],
                                      16 if STREAM_DMA[o.stream] else 1)
                for (s, v) in final_by_engine.get(eng_name, ()):
                    eng.wait_ge(sems[s], v)

            for name in ("tensor", "vector", "scalar", "gpsimd", "sync"):
                if name in per_engine or name in final_by_engine:
                    getattr(block, name)(lambda e, n=name: emit(n, e))
        self.stack.close()



def mm_group(P, psum_ap, pairs, reads, writes):
    n = len(pairs)
    for i, (lw, rv) in enumerate(pairs):
        P.op("tensor",
             lambda e, lw=lw, rv=rv, st=(i == 0), sp=(i == n - 1): e.matmul(
                 psum_ap, lw, rv, start=st, stop=sp),
             reads=reads, writes=writes)


def act(P, out, in_, func, reads, writes, bias=0.0, scale=1.0, accum_out=None):
    P.op("scalar",
         lambda e: e.activation(out, in_, func, bias=bias, scale=scale,
                                accum_out=accum_out),
         reads=reads, writes=writes)


def tt(P, out, a, b, op, reads, writes, stream="vector"):
    P.op(stream, lambda e: e.tensor_tensor(out, a, b, op), reads=reads, writes=writes)


def ts(P, out, a, s1, op, reads, writes, stream="vector", accum_out=None):
    P.op(stream, lambda e: e.tensor_scalar(out, a, s1, None, op,
                                           accum_out=accum_out),
         reads=reads, writes=writes)


def stt(P, out, in0, scalar, in1, op0, op1, reads, writes, stream="vector"):
    P.op(stream, lambda e: e.scalar_tensor_tensor(out, in0, scalar, in1, op0, op1),
         reads=reads, writes=writes)


def vcopy(P, out, in_, reads, writes, stream="vector"):
    P.op(stream, lambda e: e.tensor_copy(out, in_), reads=reads, writes=writes)




def mmop(P, out_ap, lhsT_ap, rhs_ap, reads, writes, start=True, stop=True,
         tile_position=None):
    P.op("tensor",
         lambda e, o=out_ap, l=lhsT_ap, r=rhs_ap, st=start, sp=stop,
         tp=tile_position: e.matmul(o, l, r, start=st, stop=sp,
                                    tile_position=tp),
         reads=reads, writes=writes)


def trop(P, out_ap, in_ap, ident_ap, reads, writes):
    P.op("tensor",
         lambda e, o=out_ap, i=in_ap, d=ident_ap: e.transpose(o, i, d),
         reads=reads, writes=writes)


def redop(P, out_ap, in_ap, op, reads, writes, negate=False, stream="vector"):
    P.op(stream,
         lambda e, o=out_ap, i=in_ap, op_=op, n=negate: e.tensor_reduce(
             o, i, AX.X, op_, negate=n),
         reads=reads, writes=writes)


def recipop(P, out_ap, in_ap, reads, writes):
    P.op("vector", lambda e, o=out_ap, i=in_ap: e.reciprocal(o, i),
         reads=reads, writes=writes)


def memsetop(P, ap, val, reads, writes, stream="vector"):
    P.op(stream, lambda e, a=ap, v=val: e.memset(a, v), reads=reads,
         writes=writes)


def scanop(P, out_ap, d0, d1, init, reads, writes):
    P.op("vector",
         lambda e, o=out_ap, a=d0, b=d1, i=init: e.tensor_tensor_scan(
             o, a, b, i, ALU.mult, ALU.add),
         reads=reads, writes=writes)


def tssop(P, out_ap, in_ap, scalar, op, reads, writes, stream="vector"):
    P.op(stream,
         lambda e, o=out_ap, i=in_ap, s=scalar, op_=op: e.tensor_single_scalar(
             o, i, s, op_),
         reads=reads, writes=writes)


def bcast_row(P, env, row_ap, out_sb, n, reads, tag, ps, pstag):
    for j in range(0, n, NB):
        w = min(NB, n - j)
        P.op("tensor",
             lambda e, j=j, w=w: e.matmul(ps[:, 0:w], env.ones_row[:, :],
                                          row_ap[:, j:j + w], start=True, stop=True),
             reads=list(reads) + ["c:ones_row"], writes=[pstag])
        vcopy(P, out_sb[:, j:j + w], ps[:, 0:w], reads=[pstag], writes=[tag])



def _col(v):
    assert v.shape == (v.shape[0],) and v.shape[0] % 128 == 0
    return np.ascontiguousarray(v.reshape(-1, 128).T).astype(np.float32)


def _prep_common():
    c = {}
    c["ones_row"] = np.ones((1, 128), np.float32)
    c["ones_col"] = np.ones((128, 1), np.float32)
    c["ones_col_b"] = np.ones((128, 1), BF)
    c["ident_b"] = np.eye(128, dtype=BF)
    c["ident_f"] = np.eye(128, dtype=np.float32)
    c["ident64x2"] = np.vstack([np.eye(64), np.eye(64)]).astype(BF)
    cv = np.zeros((128, 8), np.float32)
    for name, (col, val) in CONST_COLS.items():
        cv[:, col] = val
    c["cvec"] = cv
    return c








def _prep_L0(inputs):
    w = {}
    w["l0_n1c"] = _col(inputs["L0_attn_norm1_w"])
    w["l0_n2c"] = _col(inputs["L0_attn_norm2_w"])
    w["l0_inproj"] = np.ascontiguousarray(inputs["L0_attn_in_proj_w"].T)
    cw = inputs["L0_attn_conv_w"].reshape(2048, 4)
    w["l0_convw"] = np.ascontiguousarray(
        cw.reshape(16, 128, 4).transpose(1, 0, 2))
    w["l0_convb"] = _col(inputs["L0_attn_conv_b"])
    xp = inputs["L0_attn_x_proj_w"]
    w["l0_wdt"] = np.ascontiguousarray(xp[0:64].T)
    w["l0_wB"] = np.ascontiguousarray(xp[64:80].T)
    w["l0_wC"] = np.ascontiguousarray(xp[80:96].T)
    w["l0_dtw"] = np.ascontiguousarray(inputs["L0_attn_dt_proj_w"].T)
    w["l0_dtb"] = _col(inputs["L0_attn_dt_proj_b"])
    w["l0_alog"] = np.ascontiguousarray(
        inputs["L0_attn_A_log"].reshape(16, 128, 16).transpose(1, 0, 2))
    w["l0_D"] = _col(inputs["L0_attn_D"])
    w["l0_outproj"] = np.ascontiguousarray(inputs["L0_attn_out_proj_w"].T)
    w["l0_w1T"] = np.ascontiguousarray(inputs["L0_attn_w1"].T)
    w["l0_w2T"] = np.ascontiguousarray(inputs["L0_attn_w2"].T)
    w["l0_w3T"] = np.ascontiguousarray(inputs["L0_attn_w3"].T)
    selv = np.zeros((16, 16 * 128), np.float32)
    for n in range(16):
        selv[n, n * 128:(n + 1) * 128] = 1.0
    w["l0_sel"] = selv.astype(BF)
    for X in ("q", "k", "v", "f"):
        pfx = f"L0_ffn_{X}_"
        nwp = np.zeros((256, 1024), np.float32)
        nwp[:255] = inputs[pfx + "nw"]
        w[f"l0{X}_nwT"] = np.ascontiguousarray(nwp.T)
        nbp = np.zeros((1, 256), np.float32)
        nbp[0, :255] = inputs[pfx + "nb"]
        w[f"l0{X}_nb"] = nbp
        w[f"l0{X}_w1c"] = np.ascontiguousarray(
            inputs[pfx + "w1"].transpose(1, 0, 2).reshape(1024, 4096))
        w[f"l0{X}_b1c"] = _col(inputs[pfx + "b1"].reshape(4096))
        w[f"l0{X}_w2c"] = np.ascontiguousarray(
            inputs[pfx + "w2"].reshape(4096, 1024))
        w[f"l0{X}_b2"] = np.ascontiguousarray(inputs[pfx + "b2"])
    es = np.zeros((128, 32, 128), np.float32)
    for ot in range(32):
        for q in range(128):
            es[(ot * 8) % 128 + q // 16, ot, q] = 1.0
    w["l0_expsel"] = es.astype(BF)
    return w


def _prep_L1(inputs):
    w = {}
    w["l1_wqkv"] = np.ascontiguousarray(inputs["L1_attn_w_qkv"].T)
    w["l1_bqkv"] = np.ascontiguousarray(inputs["L1_attn_b_qkv"].reshape(1, 3072))
    w["l1_qg"] = np.ascontiguousarray(inputs["L1_attn_q_gamma"].reshape(1, 1024))
    w["l1_kg"] = np.ascontiguousarray(inputs["L1_attn_k_gamma"].reshape(1, 1024))
    w["l1_wout"] = np.ascontiguousarray(inputs["L1_attn_w_out"].T)
    w["l1_bout"] = _col(inputs["L1_attn_b_out"])
    half = 32
    inv = (1.0 / 10000.0 ** (np.arange(half, dtype=np.float32) / np.float32(half)))
    ang = np.arange(L, dtype=np.float32)[:, None] * inv[None, :].astype(np.float32)
    cos = np.cos(ang).astype(np.float32); sin = np.sin(ang).astype(np.float32)
    w["l1_cos"] = np.ascontiguousarray(np.tile(cos, (B_LOC, 1)))
    w["l1_sin"] = np.ascontiguousarray(np.tile(sin, (B_LOC, 1)))
    w["l1_w1T"] = np.ascontiguousarray(inputs["L1_ffn_w1"].T)
    w["l1_w2T"] = np.ascontiguousarray(inputs["L1_ffn_w2"].T)
    w["l1_b1c"] = _col(inputs["L1_ffn_b1"])
    w["l1_b2c"] = _col(inputs["L1_ffn_b2"])
    return w


def _prep_L2(inputs):
    w = {}
    w["l2_Wq"] = np.ascontiguousarray(inputs["L2_attn_Wq"])
    w["l2_Wqr"] = np.ascontiguousarray(inputs["L2_attn_Wqr"])
    w["l2_Wkr"] = np.ascontiguousarray(inputs["L2_attn_Wkr"])
    w["l2_Wkvd"] = np.ascontiguousarray(inputs["L2_attn_Wkvd"])
    w["l2_Wku"] = np.ascontiguousarray(inputs["L2_attn_Wku"])
    w["l2_Wvu"] = np.ascontiguousarray(inputs["L2_attn_Wvu"])
    w["l2_Wo"] = np.ascontiguousarray(inputs["L2_attn_Wo"])
    w["l2_lobo"] = np.ascontiguousarray(
        inputs["L2_attn_lobo_log"].reshape(1, 16).astype(np.float32))
    inv = (1.0 / 10000.0 ** (np.arange(16, dtype=np.float32) / np.float32(16.0)))
    ang = np.arange(L, dtype=np.float32)[:, None] * inv[None, :].astype(np.float32)
    cos = np.cos(ang).astype(np.float32); sin = np.sin(ang).astype(np.float32)
    w["l2_cos"] = np.ascontiguousarray(np.tile(cos, (B_LOC, 1)))
    w["l2_sin"] = np.ascontiguousarray(np.tile(sin, (B_LOC, 1)))
    mask = np.where(np.arange(128)[None, :] <= np.arange(128)[:, None],
                    0.0, -1e30).astype(BF)
    w["l2_mask"] = np.ascontiguousarray(mask)
    w["l2_WT"] = np.ascontiguousarray(inputs["L2_ffn_W"].T)
    w["l2_b"] = _col(inputs["L2_ffn_b"])
    w["l2_lu"] = np.ascontiguousarray(inputs["L2_ffn_lora_up"])
    w["l2_ld"] = np.ascontiguousarray(inputs["L2_ffn_lora_down"])
    w["l2_da"] = np.ascontiguousarray(inputs["L2_ffn_down_aux"])
    w["l2_uaT"] = np.ascontiguousarray(inputs["L2_ffn_up_aux"].T)
    return w


def _prep_L3(inputs):
    w = {}
    w["l3_uT"] = np.ascontiguousarray(inputs["L3_attn_ternary_factor_u"].T)
    w["l3_vT"] = np.ascontiguousarray(inputs["L3_attn_ternary_factor_v"].T)
    w["l3_u"] = np.ascontiguousarray(inputs["L3_attn_ternary_factor_u"])
    w["l3_v"] = np.ascontiguousarray(inputs["L3_attn_ternary_factor_v"])
    w["l3_gate"] = np.ascontiguousarray(inputs["L3_ffn_gate_w"])
    w["l3_w0T"] = np.ascontiguousarray(inputs["L3_ffn_w0"].T)
    w["l3_w1T"] = np.ascontiguousarray(inputs["L3_ffn_w1"].T)
    w["l3_b0"] = _col(inputs["L3_ffn_b0"])
    w["l3_b1"] = _col(inputs["L3_ffn_b1"])
    return w



class Env:
    pass


def slab(hbm_ap, p=128):
    return hbm_ap.rearrange("(c p) q -> p c q", p=p)


def emit_consts(P, env):
    env.ones_row = P.sbuf("c_ones_row", (1, 128), F32)
    env.ones_col = P.sbuf("c_ones_col", (128, 1), F32)
    env.ones_col_b = P.sbuf("c_ones_col_b", (128, 1), BF16)
    env.ident_b = P.sbuf("c_ident_b", (128, 128), BF16)
    env.ident_f = P.sbuf("c_ident_f", (128, 128), F32)
    env.ident64x2 = P.sbuf("c_ident64x2", (128, 64), BF16)
    env.cvec = P.sbuf("c_cvec", (128, 8), F32)
    env.cv = lambda n, p=128: env.cvec[0:p, CONST_COLS[n][0]:CONST_COLS[n][0] + 1]
    P.dma("syncq", env.ones_row[:, :], env.hbm["ones_row"], writes=["c:ones_row"])
    P.dma("syncq", env.ones_col[:, :], env.hbm["ones_col"], writes=["c:ones_col"])
    P.dma("syncq", env.ones_col_b[:, :], env.hbm["ones_col_b"], writes=["c:ones_col_b"])
    P.dma("syncq", env.ident_b[:, :], env.hbm["ident_b"], writes=["c:ident_b"])
    P.dma("syncq", env.ident_f[:, :], env.hbm["ident_f"], writes=["c:ident_f"])
    P.dma("syncq", env.ident64x2[:, :], env.hbm["ident64x2"], writes=["c:id64"])
    P.dma("syncq", env.cvec[:, :], env.hbm["cvec"], writes=["c:cvec"])


def emit_layernorm(P, env):
    xh, xnh = env.x_h, env.xn_h
    with P.phase():
        ps_mu = P.psum("ln_ps_mu", (1, NB), F32)
        ps_ms = P.psum("ln_ps_ms", (1, NB), F32)
        ps_bc = P.psum("ln_ps_bc", (128, NB), F32)
        xb = [P.sbuf(f"ln_xb{i}", (128, NB), F32) for i in range(2)]
        sq = [P.sbuf(f"ln_sq{i}", (128, NB), F32) for i in range(2)]
        rows = P.sbuf("ln_rows", (1, 4 * T), F32)
        mu = rows[:, 0 * T:1 * T]
        var = rows[:, 1 * T:2 * T]
        rstd = rows[:, 2 * T:3 * T]
        mur = rows[:, 3 * T:4 * T]
        for jc in range(T // NB):
            j = jc * NB
            for i in range(8):
                b, s_ = xb[i % 2], sq[i % 2]
                kb, ks = f"ln:xb{i % 2}", f"ln:sq{i % 2}"
                P.dma("syncq", b[:, :], xh[i * 128:(i + 1) * 128, j:j + NB],
                      writes=[kb])
                act(P, s_[:, :], b[:, :], AF.Square, reads=[kb], writes=[ks])
                P.op("tensor",
                     lambda e, b=b, i=i: e.matmul(ps_mu[:, :], env.ones_col[:, :],
                                                  b[:, :], start=(i == 0),
                                                  stop=(i == 7)),
                     reads=[kb, "c:ones_col"], writes=["ln:psmu"])
                P.op("tensor",
                     lambda e, s_=s_, i=i: e.matmul(ps_ms[:, :], env.ones_col[:, :],
                                                    s_[:, :], start=(i == 0),
                                                    stop=(i == 7)),
                     reads=[ks, "c:ones_col"], writes=["ln:psms"])
            jsl = slice(j, j + NB)
            ts(P, mu[:, jsl], ps_mu[:, :], 1.0 / D, ALU.mult,
               reads=["ln:psmu"], writes=["ln:mu"])
            ts(P, var[:, jsl], ps_ms[:, :], 1.0 / D, ALU.mult,
               reads=["ln:psms"], writes=["ln:var"])
        tt(P, rstd, mu, mu, ALU.mult, reads=["ln:mu"], writes=["ln:musq"])
        tt(P, var, var, rstd, ALU.subtract, reads=["ln:var", "ln:musq"],
           writes=["ln:var2"])
        act(P, rstd, var, AF.Ln, bias=env.cv("eps5", 1),
            reads=["ln:var2", "c:cvec"], writes=["ln:lnv"])
        act(P, rstd, rstd, AF.Exp, scale=-0.5, reads=["ln:lnv"], writes=["ln:rstd"])
        tt(P, mur, mu, rstd, ALU.mult, reads=["ln:mu", "ln:rstd"], writes=["ln:mur"])
        rstd_b = P.sbuf("ln_rstd_b", (128, T), F32)
        mur_b = P.sbuf("ln_mur_b", (128, T), F32)
        bcast_row(P, env, rstd, rstd_b, T, ["ln:rstd"], "ln:rstdb", ps_bc, "ln:psbc")
        bcast_row(P, env, mur, mur_b, T, ["ln:mur"], "ln:murb", ps_bc, "ln:psbc")
        xt = [P.sbuf(f"ln_ax{i}", (128, T), F32) for i in range(2)]
        tmp = [P.sbuf(f"ln_tmp{i}", (128, T), F32) for i in range(2)]
        xnt = [P.sbuf(f"ln_xn{i}", (128, T), BF16) for i in range(2)]
        for i in range(8):
            b, tp, xo = xt[i % 2], tmp[i % 2], xnt[i % 2]
            kb, kt, ko = f"ln:ax{i % 2}", f"ln:tmp{i % 2}", f"ln:xo{i % 2}"
            P.dma("syncq", b[:, :], xh[i * 128:(i + 1) * 128, :], writes=[kb])
            tt(P, tp[:, :], b[:, :], rstd_b[:, :], ALU.mult,
               reads=[kb, "ln:rstdb"], writes=[kt])
            tt(P, tp[:, :], tp[:, :], mur_b[:, :], ALU.subtract,
               reads=[kt, "ln:murb"], writes=[kt])
            vcopy(P, xo[:, :], tp[:, :], reads=[kt], writes=[ko])
            P.dma("scq", xnh[i * 128:(i + 1) * 128, :], xo[:, :],
                  reads=[ko], writes=[f"xn:{i}"])
            if env.layer == 0:
                P.dma("syncq", env.xn32_h[i * 128:(i + 1) * 128, :], tp[:, :],
                      reads=[kt], writes=[f"xn32:{i}"])


def emit_combine(P, env):
    with P.phase():
        xt = P.sbuf("cb_x", (128, T), F32)
        st = P.sbuf("cb_s", (128, T), F32)
        gt = P.sbuf("cb_g", (128, T), F32)
        ot = P.sbuf("cb_o", (128, T), F32)
        for i in range(8):
            sl = slice(i * 128, (i + 1) * 128)
            P.dma("syncq", xt[:, :], env.x_h[sl, :], reads=["cb:o"], writes=["cb:x"])
            P.dma("syncq", st[:, :], env.s_h[sl, :], reads=["cb:o"], writes=["cb:s"])
            P.dma("syncq", gt[:, :], env.g_h[sl, :], reads=["cb:o"], writes=["cb:g"])
            tt(P, ot[:, :], st[:, :], gt[:, :], ALU.mult,
               reads=["cb:s", "cb:g"], writes=["cb:t"])
            tt(P, ot[:, :], ot[:, :], xt[:, :], ALU.add,
               reads=["cb:t", "cb:x"], writes=["cb:o"])
            P.dma("scq", env.xout_h[sl, :], ot[:, :], reads=["cb:o"],
                  writes=[f"xo:{i}"])



def emit_L3_attn(P, env):
    C = 16
    SCALE = 0.03125
    with P.phase():
        xnb = P.sbuf("a3_xnb", (128, 8, L), BF16)
        qzu = P.sbuf("a3_qzu", (128, 8, L), BF16)
        qzv = P.sbuf("a3_qzv", (128, 8, L), BF16)
        wbuf = P.sbuf("a3_w", (128, 8, 128), BF16)
        wbuf2 = P.sbuf("a3_w2", (128, 8, 128), BF16)
        utm = P.sbuf("a3_utm", (128, 16, 64), BF16)
        vtm = P.sbuf("a3_vtm", (128, 16, 64), BF16)
        qh = [P.sbuf(f"a3_qh{i}", (128, NB), BF16) for i in range(2)]
        v1T = P.sbuf("a3_v1T", (128, 8, L), BF16)
        v2T = P.sbuf("a3_v2T", (128, 8, L), BF16)
        sout = P.sbuf("a3_s", (128, NB), F32)
        ps_a = P.psum("a3_ps_a", (128, NB), F32)
        ps_b = P.psum("a3_ps_b", (128, NB), F32)
        ps_t = P.psum("a3_ps_t", (128, 1024), BF16)
        ps_v = P.psum("a3_ps_v", (64, L), F32)

        for b in range(B_LOC):
            tok = slice(b * L, (b + 1) * L)
            for i in range(8):
                P.dma("syncq", xnb[:, i, :], env.xn_h[i * 128:(i + 1) * 128, tok],
                      reads=[f"xn:{i}"], writes=["a3:xnb"])
            for (wname, dst, dtag) in (("l3_uT", qzu, "a3:qzu"),
                                       ("l3_vT", qzv, "a3:qzv")):
                for ot in range(8):
                    P.dma("gpq", wbuf[:, :, :],
                          slab(env.hbm[wname])[:, :, ot * 128:(ot + 1) * 128],
                          reads=["a3:wbuf"], writes=["a3:wbuf"])
                    for jc in range(L // NB):
                        j = jc * NB
                        ps = ps_a if jc % 2 == 0 else ps_b
                        pk = f"a3:ps{jc % 2}"
                        mm_group(P, ps[:, :],
                                 [(wbuf[:, c, :], xnb[:, c, j:j + NB])
                                  for c in range(8)],
                                 reads=["a3:wbuf", "a3:xnb"], writes=[pk])
                        act(P, dst[:, ot, j:j + NB], ps[:, :], AF.Identity,
                            scale=SCALE, reads=[pk], writes=[dtag])
            for cc in range(C):
                til, rof = cc // 2, (cc % 2) * 64
                for stile in range(16):
                    ssl = slice(stile * 128, (stile + 1) * 128)
                    P.op("tensor",
                         lambda e, ssl=ssl: e.transpose(
                             ps_t[:, 0:64], qzu[rof:rof + 64, til, ssl],
                             env.ident64x2[rof:rof + 64, :]),
                         reads=["a3:qzu", "c:id64"], writes=["a3:pst"])
                    vcopy(P, utm[:, stile, :], ps_t[:, 0:64], reads=["a3:pst"],
                          writes=["a3:utm"])
                    P.op("tensor",
                         lambda e, ssl=ssl: e.transpose(
                             ps_t[:, 0:64], qzv[rof:rof + 64, til, ssl],
                             env.ident64x2[rof:rof + 64, :]),
                         reads=["a3:qzv", "c:id64"], writes=["a3:pst"])
                    vcopy(P, vtm[:, stile, :], ps_t[:, 0:64], reads=["a3:pst"],
                          writes=["a3:vtm"])
                for stile in range(16):
                    ssl = slice(stile * 128, (stile + 1) * 128)
                    for jc in range(L // NB):
                        j = jc * NB
                        qb = qh[jc % 2]
                        qk = f"a3:qh{jc % 2}"
                        ps = ps_a if jc % 2 == 0 else ps_b
                        pk = f"a3:ps{jc % 2}"
                        P.op("tensor",
                             lambda e, ssl=ssl, j=j, ps=ps: e.matmul(
                                 ps[:, :], qzu[rof:rof + 64, til, ssl],
                                 qzv[rof:rof + 64, til, j:j + NB],
                                 start=True, stop=True),
                             reads=["a3:qzu", "a3:qzv"], writes=[pk])
                        act(P, qb[:, :], ps[:, :], AF.Sigmoid,
                            reads=[pk], writes=[qk])
                        P.op("tensor",
                             lambda e, stile=stile, j=j, qb=qb: e.matmul(
                                 ps_v[:, j:j + NB], utm[:, stile, :], qb[:, :],
                                 start=(stile == 0), stop=(stile == 15)),
                             reads=[qk, "a3:utm"], writes=["a3:psv"])
                for jc in range(L // NB):
                    j = jc * NB
                    vcopy(P, v2T[rof:rof + 64, til, j:j + NB], ps_v[:, j:j + NB],
                          reads=["a3:psv"], writes=["a3:v2T"])
                for stile in range(16):
                    ssl = slice(stile * 128, (stile + 1) * 128)
                    for jc in range(L // NB):
                        j = jc * NB
                        qb = qh[jc % 2]
                        qk = f"a3:qh{jc % 2}"
                        ps = ps_a if jc % 2 == 0 else ps_b
                        pk = f"a3:ps{jc % 2}"
                        P.op("tensor",
                             lambda e, ssl=ssl, j=j, ps=ps: e.matmul(
                                 ps[:, :], qzv[rof:rof + 64, til, ssl],
                                 qzu[rof:rof + 64, til, j:j + NB],
                                 start=True, stop=True),
                             reads=["a3:qzu", "a3:qzv"], writes=[pk])
                        act(P, qb[:, :], ps[:, :], AF.Sigmoid,
                            reads=[pk], writes=[qk])
                        P.op("tensor",
                             lambda e, stile=stile, j=j, qb=qb: e.matmul(
                                 ps_v[:, j:j + NB], vtm[:, stile, :], qb[:, :],
                                 start=(stile == 0), stop=(stile == 15)),
                             reads=[qk, "a3:vtm"], writes=["a3:psv"])
                for jc in range(L // NB):
                    j = jc * NB
                    vcopy(P, v1T[rof:rof + 64, til, j:j + NB], ps_v[:, j:j + NB],
                          reads=["a3:psv"], writes=["a3:v1T"])
            for ot in range(8):
                osl = slice(ot * 128, (ot + 1) * 128)
                P.dma("gpq", wbuf[:, :, :], slab(env.hbm["l3_u"])[:, :, osl],
                      reads=["a3:wbuf"], writes=["a3:wbuf"])
                P.dma("gpq", wbuf2[:, :, :], slab(env.hbm["l3_v"])[:, :, osl],
                      reads=["a3:wbuf2"], writes=["a3:wbuf2"])
                for jc in range(L // NB):
                    j = jc * NB
                    ps = ps_a if jc % 2 == 0 else ps_b
                    pk = f"a3:ps{jc % 2}"
                    mm_group(P, ps[:, :],
                             [(wbuf[:, c, :], v1T[:, c, j:j + NB])
                              for c in range(8)] +
                             [(wbuf2[:, c, :], v2T[:, c, j:j + NB])
                              for c in range(8)],
                             reads=["a3:wbuf", "a3:wbuf2", "a3:v1T", "a3:v2T"],
                             writes=[pk])
                    act(P, sout[:, :], ps[:, :], AF.Identity, scale=SCALE,
                        reads=[pk], writes=["a3:sout"])
                    P.dma("scq", env.s_h[osl, b * L + j:b * L + j + NB],
                          sout[:, :], reads=["a3:sout"], writes=["a3:sh"])



def emit_L3_ffn(P, env):
    E = 8
    with P.phase():
        xnb = P.sbuf("f3_xnb", (128, 8, T), BF16)
        for i in range(8):
            P.dma("syncq", xnb[:, i, :], env.xn_h[i * 128:(i + 1) * 128, :],
                  reads=[f"xn:{i}"], writes=["f3:xnb"])
        gw = P.sbuf("f3_gw", (128, 8, E), BF16)
        P.dma("gpq", gw[:, :, :], slab(env.hbm["l3_gate"]), writes=["f3:gw"])
        ps_gate = P.psum("f3_ps_gate", (128, NB), F32)
        ps_tr = P.psum("f3_ps_tr", (128, NB), F32)
        ps_bc = P.psum("f3_ps_bc", (128, NB), F32)
        wsm = P.sbuf("f3_wsm", (128, E), F32)
        gt8 = P.sbuf("f3_gt8", (128, E), F32)
        cols = P.sbuf("f3_cols", (128, 8), F32)
        a01 = P.sbuf("f3_a01", (128, 2), F32)
        arows0 = P.sbuf("f3_arows0", (1, T), F32)
        arows1 = P.sbuf("f3_arows1", (1, T), F32)
        negmax = cols[:, 0:1]; den = cols[:, 1:2]; rec = cols[:, 2:3]
        M1 = cols[:, 3:4]; c1s = cols[:, 4:5]; cnt = cols[:, 5:6]
        m0 = cols[:, 6:7]; m1 = cols[:, 7:8]

        for tti in range(32):
            tok = slice(tti * 128, (tti + 1) * 128)
            mm_group(P, ps_gate[:, 0:E],
                     [(xnb[:, c, tok], gw[:, c, :]) for c in range(8)],
                     reads=["f3:gw", "f3:xnb"], writes=["f3:psgate"])
            P.op("vector", lambda e: e.reduce_max(negmax, ps_gate[:, 0:E], AX.X,
                                                  negate=True),
                 reads=["f3:psgate"], writes=["f3:negmax"])
            act(P, wsm[:, :], ps_gate[:, 0:E], AF.Exp, bias=negmax, accum_out=den,
                reads=["f3:psgate", "f3:negmax"], writes=["f3:wsm", "f3:den"])
            P.op("vector", lambda e: e.reciprocal(rec, den),
                 reads=["f3:den"], writes=["f3:rec"])
            ts(P, wsm[:, :], wsm[:, :], rec, ALU.mult,
               reads=["f3:wsm", "f3:rec"], writes=["f3:wsm"])
            P.op("vector", lambda e: e.reduce_max(M1, wsm[:, :], AX.X),
                 reads=["f3:wsm"], writes=["f3:M1"])
            tt(P, m0, wsm[:, 0:1], M1, ALU.is_ge, reads=["f3:wsm", "f3:M1"],
               writes=["f3:m0"])
            ts(P, gt8[:, :], wsm[:, :], wsm[:, 1:2], ALU.is_gt,
               reads=["f3:wsm"], writes=["f3:gt8"])
            P.op("vector", lambda e: e.reduce_sum(c1s, gt8[:, :], AX.X),
                 reads=["f3:gt8"], writes=["f3:c1s"])
            tt(P, cnt, wsm[:, 0:1], wsm[:, 1:2], ALU.is_equal, reads=["f3:wsm"],
               writes=["f3:cnt0"])
            tt(P, cnt, cnt, c1s, ALU.add, reads=["f3:cnt0", "f3:c1s"],
               writes=["f3:cnt"])
            P.op("vector", lambda e: e.tensor_single_scalar(m1, cnt, 1.0,
                                                            ALU.is_equal),
                 reads=["f3:cnt"], writes=["f3:m1"])
            tt(P, a01[:, 0:1], wsm[:, 0:1], m0, ALU.mult,
               reads=["f3:wsm", "f3:m0"], writes=["f3:a01"])
            tt(P, a01[:, 1:2], wsm[:, 1:2], m1, ALU.mult,
               reads=["f3:wsm", "f3:m1"], writes=["f3:a01"])
            P.op("tensor", lambda e: e.transpose(ps_tr[0:1, 0:128],
                                                 a01[:, 0:1], env.ident_f[:, :]),
                 reads=["f3:a01", "c:ident_f"], writes=["f3:pstr"])
            vcopy(P, arows0[:, tok], ps_tr[0:1, 0:128], reads=["f3:pstr"],
                  writes=["f3:arows"])
            P.op("tensor", lambda e: e.transpose(ps_tr[0:1, 128:256],
                                                 a01[:, 1:2], env.ident_f[:, :]),
                 reads=["f3:a01", "c:ident_f"], writes=["f3:pstr"])
            vcopy(P, arows1[:, tok], ps_tr[0:1, 128:256], reads=["f3:pstr"],
                  writes=["f3:arows"])
        a0b = P.sbuf("f3_a0b", (128, T), F32)
        a1b = P.sbuf("f3_a1b", (128, T), F32)
        bcast_row(P, env, arows0[:, :], a0b, T, ["f3:arows"], "f3:a0b",
                  ps_bc, "f3:psbc")
        bcast_row(P, env, arows1[:, :], a1b, T, ["f3:arows"], "f3:a1b",
                  ps_bc, "f3:psbc")
        w0 = P.sbuf("f3_w0", (128, 8, 128), BF16)
        w1 = P.sbuf("f3_w1", (128, 8, 128), BF16)
        b0c = P.sbuf("f3_b0", (128, 8), F32)
        b1c = P.sbuf("f3_b1", (128, 8), F32)
        P.dma("syncq", b0c[:, :], env.hbm["l3_b0"], writes=["f3:b0"])
        P.dma("syncq", b1c[:, :], env.hbm["l3_b1"], writes=["f3:b1"])
        ps_e0 = P.psum("f3_ps_e0", (128, NB), F32)
        ps_e1 = P.psum("f3_ps_e1", (128, NB), F32)
        e0 = P.sbuf("f3_e0", (128, NB), F32)
        e1 = P.sbuf("f3_e1", (128, NB), F32)
        gt_ = P.sbuf("f3_g", (128, NB), F32)
        for ot in range(8):
            osl = slice(ot * 128, (ot + 1) * 128)
            P.dma("gpq", w0[:, :, :], slab(env.hbm["l3_w0T"])[:, :, osl],
                  reads=["f3:w0"], writes=["f3:w0"])
            P.dma("gpq", w1[:, :, :], slab(env.hbm["l3_w1T"])[:, :, osl],
                  reads=["f3:w1"], writes=["f3:w1"])
            for j in range(0, T, NB):
                mm_group(P, ps_e0[:, :],
                         [(w0[:, c, :], xnb[:, c, j:j + NB]) for c in range(8)],
                         reads=["f3:w0", "f3:xnb"], writes=["f3:pse0"])
                mm_group(P, ps_e1[:, :],
                         [(w1[:, c, :], xnb[:, c, j:j + NB]) for c in range(8)],
                         reads=["f3:w1", "f3:xnb"], writes=["f3:pse1"])
                act(P, e0[:, :], ps_e0[:, :], AF.Silu, bias=b0c[:, ot:ot + 1],
                    reads=["f3:pse0", "f3:b0"], writes=["f3:e0"])
                act(P, e1[:, :], ps_e1[:, :], AF.Silu, bias=b1c[:, ot:ot + 1],
                    reads=["f3:pse1", "f3:b1"], writes=["f3:e1"])
                tt(P, e0[:, :], e0[:, :], a0b[:, j:j + NB], ALU.mult,
                   reads=["f3:e0", "f3:a0b"], writes=["f3:e0"])
                tt(P, e1[:, :], e1[:, :], a1b[:, j:j + NB], ALU.mult,
                   reads=["f3:e1", "f3:a1b"], writes=["f3:e1"])
                tt(P, gt_[:, :], e0[:, :], e1[:, :], ALU.add,
                   reads=["f3:e0", "f3:e1"], writes=["f3:g1"])
                act(P, gt_[:, :], gt_[:, :], AF.Silu, reads=["f3:g1"],
                    writes=["f3:g2"])
                P.dma("scq", env.g_h[osl, j:j + NB], gt_[:, :],
                      reads=["f3:g2"], writes=["f3:gh"])





def emit_L2_attn(P, env):
    H, DH, DR = 16, 64, 32
    SC = 1.0 / float(np.sqrt(96.0))
    with P.phase():
        latent = P.sbuf("a2_lat", (128, 2, L), BF16)
        qc = P.sbuf("a2_qc", (128, 8, L), BF16)
        kc = P.sbuf("a2_kc", (128, 8, L), BF16)
        qrp = P.sbuf("a2_qrp", (64, 8, L), BF16)
        kr2 = P.sbuf("a2_kr2", (64, L), BF16)
        vtm = P.sbuf("a2_vtm", (128, 16, 1024), BF16)
        lobo = P.sbuf("a2_lobo", (1, 16), F32)
        Cbr = P.sbuf("a2_Cbr", (1, 16), F32)
        Cbc = P.sbuf("a2_Cbc", (128, 16), F32)
        ps_sc = P.psum("a2_ps_sc", (128, 2048), F32)
        ps_pt = P.psum("a2_ps_pt", (128, 1024), BF16)
        ps_o = P.psum("a2_ps_o", (128, NB), F32)
        P.dma("syncq", lobo[:, :], env.hbm["l2_lobo"], writes=["a2:lobo"])
        act(P, Cbr[:, :], lobo[:, :], AF.Exp, reads=["a2:lobo"], writes=["a2:Cbr"])
        maskt = P.sbuf("a2_mask", (128, 128), BF16)
        P.dma("syncq", maskt[:, :], env.hbm["l2_mask"], writes=["a2:mask"])

        for b in range(B_LOC):
            tok = slice(b * L, (b + 1) * L)
            with P.phase():
                ps_pr = ps_sc
                if b == 0:
                    bcast_row(P, env, Cbr[:, :], Cbc, 16, ["a2:Cbr"], "a2:Cbc",
                              ps_pr[:, 0:NB], "a2:pspr0")
                xnb = P.sbuf("a2_xnb", (128, 8, L), BF16)
                for i in range(8):
                    P.dma("syncq", xnb[:, i, :],
                          env.xn_h[i * 128:(i + 1) * 128, tok],
                          reads=[f"xn:{i}"], writes=["a2:xnb"])
                cosb = P.sbuf("a2_cos", (128, 16, 16), F32)
                sinb = P.sbuf("a2_sin", (128, 16, 16), F32)
                P.dma("syncq", cosb[:, :, :],
                      env.hbm["l2_cos"].rearrange("(t p) c -> p t c", p=128)
                      [:, b * 16:(b + 1) * 16, :], writes=["a2:cos"])
                P.dma("syncq", sinb[:, :, :],
                      env.hbm["l2_sin"].rearrange("(t p) c -> p t c", p=128)
                      [:, b * 16:(b + 1) * 16, :], writes=["a2:sin"])
                wkvd = P.sbuf("a2_wkvd", (128, 8, 256), BF16)
                P.dma("gpq", wkvd[:, :, :], slab(env.hbm["l2_Wkvd"]),
                      writes=["a2:wkvd"])
                for o2 in range(2):
                    for jc in range(4):
                        j = jc * NB
                        pslc = ps_pr[:, (jc % 2) * NB:(jc % 2) * NB + NB]
                        pk = f"a2:pspr{jc % 2}"
                        mm_group(P, pslc,
                                 [(wkvd[:, c, o2 * 128:(o2 + 1) * 128],
                                   xnb[:, c, j:j + NB]) for c in range(8)],
                                 reads=["a2:wkvd", "a2:xnb"], writes=[pk])
                        act(P, latent[:, o2, j:j + NB], pslc, AF.Identity,
                            reads=[pk], writes=["a2:lat"])
                wq = P.sbuf("a2_wq", (128, 8, 128), BF16)
                for ot in range(8):
                    P.dma("gpq", wq[:, :, :],
                          slab(env.hbm["l2_Wq"])[:, :, ot * 128:(ot + 1) * 128],
                          reads=["a2:wq"], writes=["a2:wq"])
                    for jc in range(4):
                        j = jc * NB
                        pslc = ps_pr[:, (jc % 2) * NB:(jc % 2) * NB + NB]
                        pk = f"a2:pspr{jc % 2}"
                        mm_group(P, pslc,
                                 [(wq[:, c, :], xnb[:, c, j:j + NB])
                                  for c in range(8)],
                                 reads=["a2:wq", "a2:xnb"], writes=[pk])
                        act(P, qc[:, ot, j:j + NB], pslc, AF.Identity,
                            reads=[pk], writes=["a2:qc"])
                wku = P.sbuf("a2_wku", (128, 2, 1024), BF16)
                P.dma("gpq", wku[:, :, :], slab(env.hbm["l2_Wku"]),
                      writes=["a2:wku"])
                for ot in range(8):
                    for jc in range(4):
                        j = jc * NB
                        pslc = ps_pr[:, (jc % 2) * NB:(jc % 2) * NB + NB]
                        pk = f"a2:pspr{jc % 2}"
                        mm_group(P, pslc,
                                 [(wku[:, c2, ot * 128:(ot + 1) * 128],
                                   latent[:, c2, j:j + NB]) for c2 in range(2)],
                                 reads=["a2:wku", "a2:lat"], writes=[pk])
                        act(P, kc[:, ot, j:j + NB], pslc, AF.Identity,
                            reads=[pk], writes=["a2:kc"])
                wvu = P.sbuf("a2_wvu", (128, 2, 1024), BF16)
                P.dma("gpq", wvu[:, :, :], slab(env.hbm["l2_Wvu"]),
                      writes=["a2:wvu"])
                for tt_ in range(16):
                    tsl = slice(tt_ * 128, (tt_ + 1) * 128)
                    for hc in range(2):
                        pslc = ps_pr[:, hc * NB:(hc + 1) * NB]
                        pk = f"a2:pspr{hc}"
                        mm_group(P, pslc,
                                 [(latent[:, c2, tsl],
                                   wvu[:, c2, hc * NB:(hc + 1) * NB])
                                  for c2 in range(2)],
                                 reads=["a2:wvu", "a2:lat"], writes=[pk])
                        act(P, vtm[:, tt_, hc * NB:(hc + 1) * NB], pslc,
                            AF.Identity, reads=[pk], writes=["a2:vtm"])
                wqr = P.sbuf("a2_wqr", (128, 8, 256), BF16)
                wkr = P.sbuf("a2_wkr", (128, 8, 32), BF16)

                P.dma("gpq", wkr[:, :, :], slab(env.hbm["l2_Wkr"]),
                      writes=["a2:wkr"])
                qrt = P.sbuf("a2_qrt", (128, 512), F32)
                krt_all = P.sbuf("a2_krta", (128, 16, 32), F32)
                krr_all = P.sbuf("a2_krra", (128, 16, 32), BF16)
                qrr = P.sbuf("a2_qrr", (128, 512), BF16)
                krr = P.sbuf("a2_krr", (128, 64), BF16)
                tmp1 = P.sbuf("a2_tmp1", (128, 256), F32)
                tmp2 = P.sbuf("a2_tmp2", (128, 256), F32)

                def rope_halfrot(src, dst, nh, tt_, rk, wk):
                    s3 = src.rearrange("p (h d) -> p h d", h=nh)
                    d3 = dst.rearrange("p (h d) -> p h d", h=nh)
                    cosa = cosb[:, tt_, :].unsqueeze(1).broadcast_to((128, nh, 16))
                    sina = sinb[:, tt_, :].unsqueeze(1).broadcast_to((128, nh, 16))
                    m1 = tmp1[:, 0:nh * 16].rearrange("p (h d) -> p h d", h=nh)
                    m2 = tmp2[:, 0:nh * 16].rearrange("p (h d) -> p h d", h=nh)
                    t1 = s3[:, :, 0:16]; t2 = s3[:, :, 16:32]
                    tt(P, m1, t1, cosa, ALU.mult, reads=[rk, "a2:cos"],
                       writes=["a2:t1"])
                    tt(P, m2, t2, sina, ALU.mult, reads=[rk, "a2:sin"],
                       writes=["a2:t2"])
                    tt(P, d3[:, :, 0:16], m1, m2, ALU.subtract,
                       reads=["a2:t1", "a2:t2"], writes=[wk])
                    tt(P, m1, t2, cosa, ALU.mult, reads=[rk, "a2:cos"],
                       writes=["a2:t1"])
                    tt(P, m2, t1, sina, ALU.mult, reads=[rk, "a2:sin"],
                       writes=["a2:t2"])
                    tt(P, d3[:, :, 16:32], m1, m2, ALU.add,
                       reads=["a2:t1", "a2:t2"], writes=[wk])

                for tt_ in range(16):
                    tsl = slice(tt_ * 128, (tt_ + 1) * 128)
                    pslc = ps_pr[:, 0:NB]
                    for hf in range(2):
                        P.dma("gpq", wqr[:, :, :],
                              slab(env.hbm["l2_Wqr"])[:, :, hf * 256:(hf + 1) * 256],
                              reads=["a2:wqr"], writes=["a2:wqr"])
                        mm_group(P, pslc[:, hf * 256:(hf + 1) * 256]
                                 if False else ps_pr[:, hf * 256:(hf + 1) * 256],
                                 [(xnb[:, c, tsl], wqr[:, c, :]) for c in range(8)],
                                 reads=["a2:wqr", "a2:xnb"], writes=["a2:pspr0"])
                    vcopy(P, qrt[:, :], pslc, reads=["a2:pspr0"], writes=["a2:qrt"])
                    rope_halfrot(qrt[:, :], qrr[:, :], 16, tt_, "a2:qrt", "a2:ro")
                    for pr in range(8):
                        trop(P, ps_pt[0:64, 0:128],
                             qrr[:, pr * 64:(pr + 1) * 64],
                             env.ident_b[:, :],
                             reads=["a2:ro", "c:ident_b"], writes=["a2:pspt"])
                        vcopy(P, qrp[:, pr, tsl], ps_pt[0:64, 0:128],
                              reads=["a2:pspt"], writes=["a2:qrp"])
                    pslc2 = ps_pr[:, NB:NB + 32]
                    mm_group(P, pslc2,
                             [(xnb[:, c, tsl], wkr[:, c, :]) for c in range(8)],
                             reads=["a2:wkr", "a2:xnb"], writes=["a2:pspr1"])
                    vcopy(P, krt_all[:, tt_, :], pslc2, reads=["a2:pspr1"],
                          writes=["a2:krt"])
                s3 = krt_all[:, :, :]
                d3 = krr_all[:, :, :]
                ca = cosb[:, :, :]
                sa = sinb[:, :, :]
                m1 = tmp1[:, 0:256].rearrange("p (h d) -> p h d", h=16)
                m2 = tmp2[:, 0:256].rearrange("p (h d) -> p h d", h=16)
                t1 = s3[:, :, 0:16]; t2 = s3[:, :, 16:32]
                tt(P, m1, t1, ca, ALU.mult, reads=["a2:krt", "a2:cos"],
                   writes=["a2:t1"])
                tt(P, m2, t2, sa, ALU.mult, reads=["a2:krt", "a2:sin"],
                   writes=["a2:t2"])
                tt(P, d3[:, :, 0:16], m1, m2, ALU.subtract,
                   reads=["a2:t1", "a2:t2"], writes=["a2:kro"])
                tt(P, m1, t2, ca, ALU.mult, reads=["a2:krt", "a2:cos"],
                   writes=["a2:t1"])
                tt(P, m2, t1, sa, ALU.mult, reads=["a2:krt", "a2:sin"],
                   writes=["a2:t2"])
                tt(P, d3[:, :, 16:32], m1, m2, ALU.add,
                   reads=["a2:t1", "a2:t2"], writes=["a2:kro"])
                for tt_ in range(16):
                    tsl = slice(tt_ * 128, (tt_ + 1) * 128)
                    vcopy(P, krr[:, 0:32], krr_all[:, tt_, :], reads=["a2:kro"],
                          writes=["a2:kroc"])
                    vcopy(P, krr[:, 32:64], krr_all[:, tt_, :], reads=["a2:kro"],
                          writes=["a2:kroc"])
                    trop(P, ps_pt[0:64, 128:256], krr[:, 0:64],
                         env.ident_b[:, :],
                         reads=["a2:kroc", "c:ident_b"], writes=["a2:pspt"])
                    vcopy(P, kr2[:, tsl], ps_pt[0:64, 128:256],
                          reads=["a2:pspt"], writes=["a2:kr2"])
                if env.dbg and b == 0:
                    P.dma("scq", env.krr_dbg[:, :, :], krr_all[:, :, :],
                          reads=["a2:kro"], writes=["dbg:krr"])
            if env.dbg and b == 0:
                P.dma("scq", env.kr_dbg, kr2[:, :], reads=["a2:kr2"],
                      writes=["dbg:kr"])
                P.dma("scq", env.qr_dbg,
                      qrp[:, :, :].rearrange("p a q -> p (a q)"),
                      reads=["a2:qrp"], writes=["dbg:qr"])
            with P.phase():
              ostg = P.sbuf("a2_ostg", (64, 16, 128), BF16)
              probsdbg_buf = P.sbuf("a2_dbgbuf", (128, 2048), F32)
              probs = P.sbuf("a2_probs", (128, 2048), BF16)
              probsT = P.sbuf("a2_probsT", (128, 16, 128), BF16)
              cols = P.sbuf("a2_cols", (128, 4), F32)
              negmax = cols[:, 0:1]; den = cols[:, 1:2]; rec = cols[:, 2:3]
              for qt in range(16):
                  qsl = slice(qt * 128, (qt + 1) * 128)
                  width = (qt + 1) * 128
                  nch = (width + NB - 1) // NB
                  for h in range(H):
                      til, rof = h // 2, 64 * (h % 2)
                      r32 = 32 * (h % 2)
                      for ch in range(nch):
                          cw = min(NB, width - ch * NB)
                          ksl = slice(ch * NB, ch * NB + cw)
                          pslc = ps_sc[:, ch * NB:ch * NB + cw]
                          mmop(P, pslc, qc[rof:rof + 64, til, qsl],
                               kc[rof:rof + 64, til, ksl],
                               reads=["a2:qc", "a2:kc"], writes=["a2:pssc"],
                               start=True, stop=False)
                          last = (ch == nch - 1)
                          mmop(P, pslc, qrp[r32:r32 + 32, h // 2, qsl],
                               kr2[r32:r32 + 32, ksl],
                               reads=["a2:qrp", "a2:kr2"], writes=["a2:pssc"],
                               start=False, stop=(not last))
                          if last:
                              mmop(P, ps_sc[:, width - 128:width],
                                   env.ident_b[:, :], maskt[:, :],
                                   reads=["c:ident_b", "a2:mask"],
                                   writes=["a2:pssc"], start=False, stop=True)
                      if env.dbg and b == 0 and h == 0 and qt < 4:
                          scf = probsdbg_buf
                          vcopy(P, scf[:, 0:width], ps_sc[:, 0:width],
                                reads=["a2:pssc"], writes=["a2:dbgbuf"])
                          P.dma("scq", env.sc_dbg[:, qt * 2048:qt * 2048 + width],
                                scf[:, 0:width], reads=["a2:dbgbuf"], writes=["a2:scd"])
                      redop(P, negmax, ps_sc[:, 0:width], ALU.max,
                            reads=["a2:pssc"], writes=["a2:negmax"],
                            negate=True)
                      ts(P, negmax, negmax, SC, ALU.mult,
                         reads=["a2:negmax"], writes=["a2:negmax"])
                      act(P, probs[:, 0:width], ps_sc[:, 0:width], AF.Exp,
                          scale=SC, bias=negmax, accum_out=den,
                          reads=["a2:pssc", "a2:negmax"],
                          writes=["a2:probs", "a2:den"])
                      if env.dbg and b == 0 and qt == 0:
                          P.dma("scq", env.den_dbg[:, h:h + 1], den,
                                reads=["a2:den"], writes=["dbg:den"],
                                allow_slow_non_contiguous=True)
                      if env.dbg and b == 0 and qt < 2:
                          P.dma("scq", env.nm_dbg[:, qt * 16 + h:qt * 16 + h + 1],
                                negmax, reads=["a2:negmax"], writes=["dbg:nm"],
                                allow_slow_non_contiguous=True)
                      tt(P, den, den, Cbc[:, h:h + 1], ALU.add,
                         reads=["a2:den", "a2:Cbc"], writes=["a2:den"])
                      if b == 0 and qt == 0 and h == 0:
                          P.dma("scq", env.cbc_dbg, Cbc[:, :],
                                reads=["a2:Cbc"], writes=["dbg:cbc"])
                      recipop(P, rec, den, reads=["a2:den"],
                              writes=["a2:rec"])
                      ts(P, probs[:, 0:width], probs[:, 0:width], rec, ALU.mult,
                         reads=["a2:probs", "a2:rec"], writes=["a2:probs"])
                      if env.dbg and b == 0 and h == 0 and qt < 4:
                          prf = probsdbg_buf
                          vcopy(P, prf[:, 0:width], probs[:, 0:width],
                                reads=["a2:probs"], writes=["a2:dbgbuf"])
                          P.dma("scq", env.pr_dbg[:, qt * 2048:qt * 2048 + width],
                                prf[:, 0:width], reads=["a2:dbgbuf"], writes=["a2:prd"])
                      for kt in range(qt + 1):
                          trop(P, ps_pt[:, 0:128],
                               probs[:, kt * 128:(kt + 1) * 128],
                               env.ident_b[:, :],
                               reads=["a2:probs", "c:ident_b"],
                               writes=["a2:pspt"])
                          vcopy(P, probsT[:, kt, :], ps_pt[:, 0:128],
                                reads=["a2:pspt"], writes=["a2:probsT"])
                      for kt in range(qt + 1):
                          mmop(P, ps_o[0:64, 0:128],
                               vtm[:, kt, h * 64:h * 64 + 64],
                               probsT[:, kt, :],
                               reads=["a2:vtm", "a2:probsT"],
                               writes=["a2:pso"], start=(kt == 0),
                               stop=(kt == qt))
                      act(P, ostg[:, h, :], ps_o[0:64, 0:128],
                          AF.Identity, reads=["a2:pso"], writes=["a2:ostg"])
                  P.dma("scq", env.o_h[:, b * L + qt * 128:b * L + (qt + 1) * 128]
                        .rearrange("(h p) q -> p h q", p=64),
                        ostg[:, :, :], reads=["a2:ostg"], writes=["a2:oh"])
            with P.phase():
                wo = P.sbuf("a2_wo", (128, 8, 128), BF16)
                och = P.sbuf("a2_och", (128, 8, NB), BF16)
                ps_pr = ps_sc
                sout = P.sbuf("a2_sout", (128, NB), F32)
                for jc in range(4):
                    j = jc * NB
                    P.dma("syncq", och[:, :, :],
                          env.o_h.rearrange("(c p) q -> p c q", p=128)
                          [:, :, b * L + j:b * L + j + NB],
                          reads=["a2:och", "a2:oh"], writes=["a2:och"])
                    for ot in range(8):
                        P.dma("gpq", wo[:, :, :],
                              slab(env.hbm["l2_Wo"])[:, :, ot * 128:(ot + 1) * 128],
                              reads=["a2:wo"], writes=["a2:wo"])
                        pslc = ps_pr[:, (ot % 2) * NB:(ot % 2) * NB + NB]
                        pk = f"a2:pspr{ot % 2}"
                        mm_group(P, pslc,
                                 [(wo[:, c, :], och[:, c, :]) for c in range(8)],
                                 reads=["a2:wo", "a2:och"], writes=[pk])
                        act(P, sout[:, :], pslc, AF.Identity, reads=[pk],
                            writes=["a2:sout"])
                        P.dma("scq",
                              env.s_h[ot * 128:(ot + 1) * 128,
                                      b * L + j:b * L + j + NB],
                              sout[:, :], reads=["a2:sout"], writes=["a2:sh"])



def emit_L2_ffn(P, env):
    with P.phase():
        xnb = P.sbuf("f2_xnb", (128, 8, T), BF16)
        for i in range(8):
            P.dma("syncq", xnb[:, i, :], env.xn_h[i * 128:(i + 1) * 128, :],
                  reads=[f"xn:{i}"], writes=["f2:xnb"])
        lu = P.sbuf("f2_lu", (50, 4), F32)
        ld = P.sbuf("f2_ld", (4, 100), F32)
        da = P.sbuf("f2_da", (100, 1024), F32)
        uaT = P.sbuf("f2_uaT", (50, 1024), F32)
        a1 = P.sbuf("f2_a1", (4, 1024), F32)
        a2 = P.sbuf("f2_a2", (100, 1024), F32)
        wtb = P.sbuf("f2_wtb", (128, 1024), F32)
        weff = P.sbuf("f2_weff", (128, 8, 1024), BF16)
        bc = P.sbuf("f2_bc", (128, 8), F32)
        gt_ = P.sbuf("f2_g", (128, NB), F32)
        ps_a = P.psum("f2_ps_a", (128, 1024), F32)
        ps_g = P.psum("f2_ps_g", (128, 2 * NB), F32)
        P.dma("syncq", lu[:, :], env.hbm["l2_lu"], writes=["f2:lu"])
        P.dma("syncq", ld[:, :], env.hbm["l2_ld"], writes=["f2:ld"])
        P.dma("syncq", da[:, :], env.hbm["l2_da"], writes=["f2:da"])
        P.dma("syncq", uaT[:, :], env.hbm["l2_uaT"], writes=["f2:uaT"])
        P.dma("syncq", bc[:, :], env.hbm["l2_b"], writes=["f2:bc"])
        for jc in range(2):
            j = jc * NB
            P.op("tensor", lambda e, j=j: e.matmul(
                ps_a[:, j:j + NB][0:4, :], lu[:, :], uaT[:, j:j + NB],
                start=True, stop=True),
                 reads=["f2:lu", "f2:uaT"], writes=["f2:psa"])
        vcopy(P, a1[:, :], ps_a[0:4, 0:1024], reads=["f2:psa"], writes=["f2:a1"])
        for jc in range(2):
            j = jc * NB
            P.op("tensor", lambda e, j=j: e.matmul(
                ps_a[0:100, j:j + NB], ld[:, :], a1[:, j:j + NB],
                start=True, stop=True),
                 reads=["f2:ld", "f2:a1"], writes=["f2:psa"])
        vcopy(P, a2[:, :], ps_a[0:100, 0:1024], reads=["f2:psa"], writes=["f2:a2"])
        for ct in range(8):
            P.dma("syncq", wtb[:, :],
                  env.hbm["l2_WT"][ct * 128:(ct + 1) * 128, :],
                  reads=["f2:wtb"], writes=["f2:wtb"])
            for jc in range(2):
                j = jc * NB
                P.op("tensor", lambda e, ct=ct, j=j: e.matmul(
                    ps_a[:, j:j + NB], da[:, ct * 128:(ct + 1) * 128],
                    a2[:, j:j + NB], start=True, stop=True),
                     reads=["f2:da", "f2:a2"], writes=["f2:psa"])
            tt(P, weff[:, ct, :], ps_a[:, 0:1024], wtb[:, :], ALU.add,
               reads=["f2:psa", "f2:wtb"], writes=["f2:weff"])
        for ot in range(8):
            for jc in range(8):
                j = jc * NB
                pslc = ps_g[:, (jc % 2) * NB:(jc % 2) * NB + NB]
                pk = f"f2:psg{jc % 2}"
                mm_group(P, pslc,
                         [(weff[:, c, ot * 128:(ot + 1) * 128],
                           xnb[:, c, j:j + NB]) for c in range(8)],
                         reads=["f2:weff", "f2:xnb"], writes=[pk])
                act(P, gt_[:, :], pslc, AF.Silu, bias=bc[:, ot:ot + 1],
                    reads=[pk, "f2:bc"], writes=["f2:g"])
                P.dma("scq", env.g_h[ot * 128:(ot + 1) * 128, j:j + NB],
                      gt_[:, :], reads=["f2:g"], writes=["f2:gh"])





def emit_L1_attn(P, env):
    H = 16
    SC = 1.0 / 8.0
    with P.phase():
        qf = P.sbuf("a1_qf", (128, 8, L), BF16)
        kf = P.sbuf("a1_kf", (128, 8, L), BF16)
        vtm = P.sbuf("a1_vtm", (128, 16, 1024), BF16)
        bqb = P.sbuf("a1_bqb", (128, 3072), BF16)
        gqb = P.sbuf("a1_gqb", (128, 1024), BF16)
        gkb = P.sbuf("a1_gkb", (128, 1024), BF16)
        rowb = P.sbuf("a1_rowb", (1, 1024), F32)
        ps_sc = P.psum("a1_ps_sc", (128, 2048), F32)
        ps_pt = P.psum("a1_ps_pt", (128, 1024), BF16)
        ps_o = P.psum("a1_ps_o", (128, NB), F32)
        for kk in range(3):
            P.dma("syncq", rowb[:, :], env.hbm["l1_bqkv"][:, kk * 1024:(kk + 1) * 1024],
                  reads=["a1:rowb"], writes=["a1:rowb"])
            bcast_row(P, env, rowb[:, :], bqb[:, kk * 1024:(kk + 1) * 1024],
                      1024, ["a1:rowb"], "a1:bqb", ps_sc[:, 0:NB], "a1:ps0")
        P.dma("syncq", rowb[:, :], env.hbm["l1_qg"], reads=["a1:rowb", "a1:bqb"],
              writes=["a1:rowb"])
        bcast_row(P, env, rowb[:, :], gqb, 1024, ["a1:rowb"], "a1:gqb",
                  ps_sc[:, 0:NB], "a1:ps0")
        P.dma("syncq", rowb[:, :], env.hbm["l1_kg"], reads=["a1:rowb", "a1:gqb"],
              writes=["a1:rowb"])
        bcast_row(P, env, rowb[:, :], gkb, 1024, ["a1:rowb"], "a1:gkb",
                  ps_sc[:, 0:NB], "a1:ps0")
        wqkv = P.sbuf("a1_wqkv", (128, 8, 3072), BF16)
        for c8 in range(8):
            P.dma("gpq", wqkv[:, c8, :], slab(env.hbm["l1_wqkv"])[:, c8, :],
                  writes=["a1:wqkv"])

        for b in range(B_LOC):
            tok = slice(b * L, (b + 1) * L)
            with P.phase():
                xnb = P.sbuf("a1_xnb", (128, 8, 128), BF16)
                cosb = P.sbuf("a1_cos", (128, 16, 32), F32)
                sinb = P.sbuf("a1_sin", (128, 16, 32), F32)
                P.dma("syncq", cosb[:, :, :],
                      env.hbm["l1_cos"].rearrange("(t p) c -> p t c", p=128)
                      [:, b * 16:(b + 1) * 16, :], writes=["a1:cos"])
                P.dma("syncq", sinb[:, :, :],
                      env.hbm["l1_sin"].rearrange("(t p) c -> p t c", p=128)
                      [:, b * 16:(b + 1) * 16, :], writes=["a1:sin"])
                qkvt = P.sbuf("a1_qkvt", (128, 3072), BF16)
                ro = P.sbuf("a1_ro", (128, 1024), F32)
                m1 = P.sbuf("a1_m1", (128, 512), F32)
                m2 = P.sbuf("a1_m2", (128, 512), F32)
                sq = P.sbuf("a1_sq", (128, 1024), BF16)
                red = P.sbuf("a1_red", (128, 16), F32)
                hatb = P.sbuf("a1_hat", (128, 1024), BF16)
                _nt = int(os.environ.get("L1A_NT", "16"))
                for tt_ in range(_nt):
                    tsl = slice(tt_ * 128, (tt_ + 1) * 128)
                    P.dma("syncq", xnb[:, :, :],
                          env.xn_h.rearrange("(i p) q -> p i q", p=128)
                          [:, :, b * L + tt_ * 128:b * L + (tt_ + 1) * 128],
                          reads=["a1:xnb"] + [f"xn:{i}" for i in range(8)],
                          writes=["a1:xnb"])
                    _nog = os.environ.get("L1A_NOGEMM") == "1"
                    for jc in range(6 if not _nog else 0):
                        j = jc * NB
                        pslc = ps_sc[:, (jc % 4) * NB:(jc % 4) * NB + NB]
                        pk = f"a1:ps{jc % 4}"
                        mm_group(P, pslc,
                                 [(xnb[:, c, :], wqkv[:, c, j:j + NB])
                                  for c in range(8)],
                                 reads=["a1:wqkv", "a1:xnb"], writes=[pk])
                        if os.environ.get("L1A_NOBIAS") == "1":
                            vcopy(P, qkvt[:, j:j + NB], pslc,
                                  reads=[pk], writes=["a1:qkvt"])
                        else:
                            tt(P, qkvt[:, j:j + NB], pslc, bqb[:, j:j + NB],
                               ALU.add, reads=[pk, "a1:bqb"], writes=["a1:qkvt"])
                    if os.environ.get("L1A_NOV") != "1":
                        vcopy(P, vtm[:, tt_, :], qkvt[:, 2048:3072],
                              reads=["a1:qkvt"], writes=["a1:vtm"])
                    import os as _os
                    _lvl = int(_os.environ.get("L1A_LVL", "9"))
                    if _lvl < 1:
                        continue
                    for (base, dst, gb, dk) in ((0, qf, gqb, "a1:qf"),
                                                (1024, kf, gkb, "a1:kf")):
                        s3 = qkvt[:, base:base + 1024].rearrange(
                            "p (h j two) -> p h j two", h=16, two=2)
                        r3 = ro[:, 0:1024].rearrange(
                            "p (h j two) -> p h j two", h=16, two=2)
                        ca = cosb[:, tt_, :].unsqueeze(1).broadcast_to(
                            (128, 16, 32))
                        sa = sinb[:, tt_, :].unsqueeze(1).broadcast_to(
                            (128, 16, 32))
                        a1_ = m1[:, :].rearrange("p (h j) -> p h j", h=16)
                        a2_ = m2[:, :].rearrange("p (h j) -> p h j", h=16)
                        t1 = s3[:, :, :, 0]; t2 = s3[:, :, :, 1]
                        tt(P, a1_, t1, ca, ALU.mult,
                           reads=["a1:qkvt", "a1:cos"], writes=["a1:m1"])
                        tt(P, a2_, t2, sa, ALU.mult,
                           reads=["a1:qkvt", "a1:sin"], writes=["a1:m2"])
                        tt(P, r3[:, :, :, 0], a1_, a2_, ALU.subtract,
                           reads=["a1:m1", "a1:m2"], writes=["a1:ro"])
                        tt(P, a1_, t1, sa, ALU.mult,
                           reads=["a1:qkvt", "a1:sin"], writes=["a1:m1"])
                        tt(P, a2_, t2, ca, ALU.mult,
                           reads=["a1:qkvt", "a1:cos"], writes=["a1:m2"])
                        tt(P, r3[:, :, :, 1], a1_, a2_, ALU.add,
                           reads=["a1:m1", "a1:m2"], writes=["a1:ro"])
                        act(P, sq[:, :], ro[:, 0:1024], AF.Square,
                            reads=["a1:ro"], writes=["a1:sq"])
                        redop(P, red[:, :].unsqueeze(2),
                              sq[:, :].rearrange("p (h d) -> p h d", h=16),
                              ALU.add, reads=["a1:sq"], writes=["a1:red"])
                        act(P, red[:, :], red[:, :], AF.Ln, scale=1.0 / 64,
                            bias=env.cv("eps6"),
                            reads=["a1:red", "c:cvec"], writes=["a1:red2"])
                        act(P, red[:, :], red[:, :], AF.Exp, scale=-0.5,
                            reads=["a1:red2"], writes=["a1:rstd"])
                        tt(P, ro[:, 0:1024].rearrange("p (h d) -> p h d", h=16),
                           ro[:, 0:1024].rearrange("p (h d) -> p h d", h=16),
                           red[:, :].unsqueeze(2).broadcast_to((128, 16, 64)),
                           ALU.mult, reads=["a1:ro", "a1:rstd"],
                           writes=["a1:ro2"])
                        tt(P, hatb[:, :], ro[:, 0:1024], gb[:, :], ALU.mult,
                           reads=["a1:ro2", "a1:gqb" if base == 0 else "a1:gkb"],
                           writes=["a1:hat"])
                        if _lvl < 2:
                            continue
                        for i in range(8):
                            trop(P, ps_pt[:, 0:128],
                                 hatb[:, i * 128:(i + 1) * 128],
                                 env.ident_b[:, :],
                                 reads=["a1:hat", "c:ident_b"],
                                 writes=["a1:pspt"])
                            vcopy(P, dst[:, i, tsl], ps_pt[:, 0:128],
                                  reads=["a1:pspt"], writes=[dk])
            import os as _os
            if _os.environ.get("L1_PROJ_ONLY") == "1":
                continue
            with P.phase():
                ostg = P.sbuf("a1_ostg", (64, 16, 128), BF16)
                probs = P.sbuf("a1_probs", (128, 2048), BF16)
                probsT = P.sbuf("a1_probsT", (128, 16, 128), BF16)
                cols = P.sbuf("a1_cols", (128, 4), F32)
                negmax = cols[:, 0:1]; den = cols[:, 1:2]; rec = cols[:, 2:3]
                for qt in range(16):
                    qsl = slice(qt * 128, (qt + 1) * 128)
                    for h in range(H):
                        til, rof = h // 2, 64 * (h % 2)
                        for ch in range(4):
                            ksl = slice(ch * NB, (ch + 1) * NB)
                            mmop(P, ps_sc[:, ksl], qf[rof:rof + 64, til, qsl],
                                 kf[rof:rof + 64, til, ksl],
                                 reads=["a1:qf", "a1:kf"],
                                 writes=[f"a1:ps{ch}"])
                        redop(P, negmax, ps_sc[:, :], ALU.max,
                              reads=[f"a1:ps{c}" for c in range(4)],
                              writes=["a1:negmax"], negate=True)
                        ts(P, negmax, negmax, SC, ALU.mult,
                           reads=["a1:negmax"], writes=["a1:negmax"])
                        act(P, probs[:, :], ps_sc[:, :], AF.Exp,
                            scale=SC, bias=negmax, accum_out=den,
                            reads=[f"a1:ps{c}" for c in range(4)] + ["a1:negmax"],
                            writes=["a1:probs", "a1:den"])
                        recipop(P, rec, den, reads=["a1:den"],
                                writes=["a1:rec"])
                        ts(P, probs[:, :], probs[:, :], rec, ALU.mult,
                           reads=["a1:probs", "a1:rec"], writes=["a1:probs"])
                        for half in range(2):
                            for kt8 in range(8):
                                kt = half * 8 + kt8
                                trop(P, ps_pt[:, kt8 * 128:(kt8 + 1) * 128],
                                     probs[:, kt * 128:(kt + 1) * 128],
                                     env.ident_b[:, :],
                                     reads=["a1:probs", "c:ident_b"],
                                     writes=["a1:pspt"])
                            vcopy(P, probsT[:, half * 8:(half + 1) * 8, :]
                                  .rearrange("p a q -> p (a q)"),
                                  ps_pt[:, 0:1024], reads=["a1:pspt"],
                                  writes=["a1:probsT"])
                        for kt in range(16):
                            mmop(P, ps_o[0:64, 0:128],
                                 vtm[:, kt, h * 64:h * 64 + 64],
                                 probsT[:, kt, :],
                                 reads=["a1:vtm", "a1:probsT"],
                                 writes=["a1:pso"], start=(kt == 0),
                                 stop=(kt == 15))
                        act(P, ostg[:, h, :], ps_o[0:64, 0:128], AF.Identity,
                            reads=["a1:pso"], writes=["a1:ostg"])
                    P.dma("scq",
                          env.o_h[:, b * L + qt * 128:b * L + (qt + 1) * 128]
                          .rearrange("(h p) q -> p h q", p=64),
                          ostg[:, :, :], reads=["a1:ostg"], writes=["a1:oh"])
            with P.phase():
                wo = P.sbuf("a1_wo", (128, 8, 128), BF16)
                och = P.sbuf("a1_och", (128, 8, NB), BF16)
                boc = P.sbuf("a1_boc", (128, 8), F32)
                sout = P.sbuf("a1_sout", (128, NB), F32)
                P.dma("syncq", boc[:, :], env.hbm["l1_bout"], writes=["a1:boc"])
                for jc in range(4):
                    j = jc * NB
                    P.dma("syncq", och[:, :, :],
                          env.o_h.rearrange("(c p) q -> p c q", p=128)
                          [:, :, b * L + j:b * L + j + NB],
                          reads=["a1:och", "a1:oh"], writes=["a1:och"])
                    for ot in range(8):
                        P.dma("gpq", wo[:, :, :],
                              slab(env.hbm["l1_wout"])
                              [:, :, ot * 128:(ot + 1) * 128],
                              reads=["a1:wo"], writes=["a1:wo"])
                        pslc = ps_sc[:, (ot % 4) * NB:(ot % 4) * NB + NB]
                        pk = f"a1:ps{ot % 4}"
                        mm_group(P, pslc,
                                 [(wo[:, c, :], och[:, c, :]) for c in range(8)],
                                 reads=["a1:wo", "a1:och"], writes=[pk])
                        act(P, sout[:, :], pslc, AF.Identity,
                            bias=boc[:, ot:ot + 1],
                            reads=[pk, "a1:boc"], writes=["a1:sout"])
                        P.dma("scq",
                              env.s_h[ot * 128:(ot + 1) * 128,
                                      b * L + j:b * L + j + NB],
                              sout[:, :], reads=["a1:sout"], writes=["a1:sh"])



def emit_L1_ffn(P, env):
    with P.phase():
        alpha1 = P.sbuf("f1_al1", (128, 32), F32)
        alpha2 = P.sbuf("f1_al2", (128, 8), F32)
        b1c = P.sbuf("f1_b1c", (128, 32), F32)
        b2c = P.sbuf("f1_b2c", (128, 8), F32)
        P.dma("syncq", b1c[:, :], env.hbm["l1_b1c"], writes=["f1:b1c"])
        P.dma("syncq", b2c[:, :], env.hbm["l1_b2c"], writes=["f1:b2c"])
        with P.phase():
            wb = P.sbuf("f1_wb", (128, 8, 128), BF16)
            sb = P.sbuf("f1_sb", (128, 8, 128), BF16)
            ab = P.sbuf("f1_ab", (128, 8, 128), BF16)
            ps_a = P.psum("f1_ps_a", (128, NB), F32)
            for ot in range(32):
                P.dma("gpq", wb[:, :, :],
                      slab(env.hbm["l1_w1T"])[:, :, ot * 128:(ot + 1) * 128],
                      reads=["f1:wb"], writes=["f1:wb"])
                act(P, sb[:, :, :], wb[:, :, :], AF.Sign, reads=["f1:wb"],
                    writes=["f1:sb"])
                act(P, ab[:, :, :], wb[:, :, :], AF.Abs, reads=["f1:wb"],
                    writes=["f1:ab"])
                P.dma("scq", env.s1_h.rearrange("(c p) q -> p c q", p=128)
                      [:, :, ot * 128:(ot + 1) * 128], sb[:, :, :],
                      reads=["f1:sb"], writes=["f1:s1h"])
                mm_group(P, ps_a[:, 0:1],
                         [(ab[:, c, :], env.ones_col_b[:, :]) for c in range(8)],
                         reads=["f1:ab", "c:ones_col_b"], writes=["f1:psa"])
                ts(P, alpha1[:, ot:ot + 1], ps_a[:, 0:1], 1.0 / 1024, ALU.mult,
                   reads=["f1:psa"], writes=["f1:al1"])
            wb2 = P.sbuf("f1_wb2", (128, 32, 128), BF16)
            sb2 = P.sbuf("f1_sb2", (128, 32, 128), BF16)
            ab2 = P.sbuf("f1_ab2", (128, 32, 128), BF16)
            for ot in range(8):
                P.dma("gpq", wb2[:, :, :],
                      slab(env.hbm["l1_w2T"])[:, :, ot * 128:(ot + 1) * 128],
                      reads=["f1:wb2"], writes=["f1:wb2"])
                act(P, sb2[:, :, :], wb2[:, :, :], AF.Sign, reads=["f1:wb2"],
                    writes=["f1:sb2"])
                act(P, ab2[:, :, :], wb2[:, :, :], AF.Abs, reads=["f1:wb2"],
                    writes=["f1:ab2"])
                P.dma("scq", env.s2_h.rearrange("(c p) q -> p c q", p=128)
                      [:, :, ot * 128:(ot + 1) * 128], sb2[:, :, :],
                      reads=["f1:sb2"], writes=["f1:s2h"])
                mm_group(P, ps_a[:, 0:1],
                         [(ab2[:, c, :], env.ones_col_b[:, :]) for c in range(32)],
                         reads=["f1:ab2", "c:ones_col_b"], writes=["f1:psa"])
                ts(P, alpha2[:, ot:ot + 1], ps_a[:, 0:1], 1.0 / 4096, ALU.mult,
                   reads=["f1:psa"], writes=["f1:al2"])
        TC = 1024
        for tc in range(T // TC):
            with P.phase():
                jsl = slice(tc * TC, (tc + 1) * TC)
                xnb = P.sbuf("f1_xnb", (128, 8, TC), BF16)
                for i in range(8):
                    P.dma("syncq", xnb[:, i, :], env.xn_h[i * 128:(i + 1) * 128, jsl],
                          reads=[f"xn:{i}"], writes=["f1:xnb"])
                hb = P.sbuf("f1_hb", (128, 32, TC), BF16)
                s1b = P.sbuf("f1_s1b", (128, 8, 128), BF16)
                s2b = P.sbuf("f1_s2b", (128, 32, 128), BF16)
                gt_ = P.sbuf("f1_g", (128, NB), F32)
                ps_g = P.psum("f1_ps_g", (128, 2 * NB), F32)
                for ot in range(32):
                    P.dma("syncq", s1b[:, :, :],
                          env.s1_h.rearrange("(c p) q -> p c q", p=128)
                          [:, :, ot * 128:(ot + 1) * 128],
                          reads=["f1:s1b", "f1:s1h"], writes=["f1:s1b"])
                    for jc in range(TC // NB):
                        j = jc * NB
                        pslc = ps_g[:, (jc % 2) * NB:(jc % 2) * NB + NB]
                        pk = f"f1:psg{jc % 2}"
                        mm_group(P, pslc,
                                 [(s1b[:, c, :], xnb[:, c, j:j + NB])
                                  for c in range(8)],
                                 reads=["f1:s1b", "f1:xnb"], writes=[pk])
                        act(P, hb[:, ot, j:j + NB], pslc, AF.Gelu,
                            scale=alpha1[:, ot:ot + 1], bias=b1c[:, ot:ot + 1],
                            reads=[pk, "f1:al1", "f1:b1c"], writes=["f1:hb"])
                for dt_ in range(8):
                    P.dma("syncq", s2b[:, :, :],
                          env.s2_h.rearrange("(c p) q -> p c q", p=128)
                          [:, :, dt_ * 128:(dt_ + 1) * 128],
                          reads=["f1:s2b", "f1:s2h"], writes=["f1:s2b"])
                    for jc in range(TC // NB):
                        j = jc * NB
                        pslc = ps_g[:, (jc % 2) * NB:(jc % 2) * NB + NB]
                        pk = f"f1:psg{jc % 2}"
                        mm_group(P, pslc,
                                 [(s2b[:, c, :], hb[:, c, j:j + NB])
                                  for c in range(32)],
                                 reads=["f1:s2b", "f1:hb"], writes=[pk])
                        act(P, gt_[:, :], pslc, AF.Silu,
                            scale=alpha2[:, dt_:dt_ + 1], bias=b2c[:, dt_:dt_ + 1],
                            reads=[pk, "f1:al2", "f1:b2c"], writes=["f1:g"])
                        P.dma("scq",
                              env.g_h[dt_ * 128:(dt_ + 1) * 128,
                                      tc * TC + j:tc * TC + j + NB],
                              gt_[:, :], reads=["f1:g"], writes=["f1:gh"])





def _rows_stats(P, env, src_tiles_loader, nt, ps_mu, ps_ms, mu_row, msq_row,
                tag, want_mu=True):
    pass


def emit_L0_attn(P, env):
    DI = 2048
    with P.phase():
        n1c = P.sbuf("m_n1c", (128, 8), F32)
        n2c = P.sbuf("m_n2c", (128, 8), F32)
        convw = P.sbuf("m_convw", (128, 16, 4), F32)
        convb = P.sbuf("m_convb", (128, 16), F32)
        dtb = P.sbuf("m_dtb", (128, 16), F32)
        Dc = P.sbuf("m_Dc", (128, 16), F32)
        alog = P.sbuf("m_alog", (128, 16, 16), F32)
        As = P.sbuf("m_As", (128, 16, 16), F32)
        sel = P.sbuf("m_sel", (16, 2048), BF16)
        P.dma("syncq", n1c[:, :], env.hbm["l0_n1c"], writes=["m:n1c"])
        P.dma("syncq", n2c[:, :], env.hbm["l0_n2c"], writes=["m:n2c"])
        P.dma("syncq", convw[:, :, :], env.hbm["l0_convw"], writes=["m:convw"])
        P.dma("syncq", convb[:, :], env.hbm["l0_convb"], writes=["m:convb"])
        P.dma("syncq", dtb[:, :], env.hbm["l0_dtb"], writes=["m:dtb"])
        P.dma("syncq", Dc[:, :], env.hbm["l0_D"], writes=["m:Dc"])
        P.dma("syncq", alog[:, :, :], env.hbm["l0_alog"], writes=["m:alog"])
        P.dma("syncq", sel[:, :], env.hbm["l0_sel"], writes=["m:sel"])
        act(P, As[:, :, :], alog[:, :, :], AF.Exp, reads=["m:alog"],
            writes=["m:As0"])
        P.op("vector", lambda e, a=As: e.tensor_scalar_mul(a[:, :, :],
                                                           a[:, :, :], -1.0),
             reads=["m:As0"], writes=["m:As"])

        _ph = int(os.environ.get("L0_PH", "9"))
        for b in range(B_LOC):
            tok = slice(b * L, (b + 1) * L)
            with P.phase():
                ps2 = P.psum("m_ps2", (128, 1024), F32)
                ps_ms = P.psum("m_ps_ms", (1, NB), F32)
                xnb = P.sbuf("m_xnb", (128, 8, L), BF16)
                for i in range(8):
                    P.dma("syncq", xnb[:, i, :],
                          env.xn_h[i * 128:(i + 1) * 128, tok],
                          reads=[f"xn:{i}"], writes=["m:xnb"])
                sq = P.sbuf("m_sq", (128, NB), F32)
                rows = P.sbuf("m_rows", (1, L), F32)
                rstdb = P.sbuf("m_rstdb", (128, L), F32)
                xr = P.sbuf("m_xr", (128, 8, L), BF16)
                for jc in range(4):
                    j = jc * NB
                    for i in range(8):
                        act(P, sq[:, :], xnb[:, i, j:j + NB], AF.Square,
                            reads=["m:xnb"], writes=["m:sq"])
                        mmop(P, ps_ms[:, :], env.ones_col[:, :], sq[:, :],
                             reads=["m:sq", "c:ones_col"], writes=["m:psms"],
                             start=(i == 0), stop=(i == 7))
                    act(P, rows[:, j:j + NB], ps_ms[:, :], AF.Ln,
                        scale=1.0 / D, bias=env.cv("eps5", 1),
                        reads=["m:psms", "c:cvec"], writes=["m:lnr"])
                    act(P, rows[:, j:j + NB], rows[:, j:j + NB], AF.Exp,
                        scale=-0.5, reads=["m:lnr"], writes=["m:rstd"])
                bcast_row(P, env, rows[:, :], rstdb, L, ["m:rstd"], "m:rstdb",
                          ps2[:, 0:NB], "m:ps0")
                for i in range(8):
                    stt(P, xr[:, i, :], xnb[:, i, :], n1c[:, i:i + 1],
                        rstdb[:, :], ALU.mult, ALU.mult,
                        reads=["m:xnb", "m:n1c", "m:rstdb"], writes=["m:xr"])
                if env.dbg:
                    for i in range(8):
                        P.dma("scq", env.xr_dbg[i * 128:(i + 1) * 128, tok],
                              xr[:, i, :], reads=["m:xr"], writes=["dbg:xr"])
                    P.dma("scq", env.rs_dbg[:, tok], rstdb[:, :],
                          reads=["m:rstdb"], writes=["dbg:rs"])
                upad = P.sbuf("m_upad", (128, 16, 3 + L), BF16)
                wsl = P.sbuf("m_wsl", (128, 8, 128), BF16)
                zst = P.sbuf("m_zst", (128, NB), BF16)
                memsetop(P, upad[:, :, :], 0.0, reads=[], writes=["m:upad"])
                for ot in range(32):
                    P.dma("gpq", wsl[:, :, :],
                          slab(env.hbm["l0_inproj"])[:, :, ot * 128:(ot + 1) * 128],
                          reads=["m:wsl"], writes=["m:wsl"])
                    for jc in range(4):
                        j = jc * NB
                        pslc = ps2[:, (jc % 2) * NB:(jc % 2) * NB + NB]
                        pk = f"m:ps{jc % 2}"
                        mm_group(P, pslc,
                                 [(wsl[:, c, :], xr[:, c, j:j + NB])
                                  for c in range(8)],
                                 reads=["m:wsl", "m:xr"], writes=[pk])
                        if ot < 16:
                            act(P, upad[:, ot, 3 + j:3 + j + NB], pslc,
                                AF.Identity, reads=[pk], writes=["m:upad"])
                        else:
                            act(P, zst[:, :], pslc, AF.Identity, reads=[pk],
                                writes=["m:zst"])
                            P.dma("scq",
                                  env.z_h[(ot - 16) * 128:(ot - 15) * 128,
                                          b * L + j:b * L + j + NB],
                                  zst[:, :], reads=["m:zst"], writes=["m:zh"])
                uact = P.sbuf("m_uact", (128, NB), F32)
                uab = P.sbuf("m_uab", (128, NB), BF16)
                for ot in range(16):
                    for jc in range(4):
                        j = jc * NB
                        ts(P, uact[:, :], upad[:, ot, j:j + NB],
                           convw[:, ot, 0:1], ALU.mult,
                           reads=["m:upad", "m:convw"], writes=["m:uact"])
                        for k in range(1, 4):
                            stt(P, uact[:, :], upad[:, ot, k + j:k + j + NB],
                                convw[:, ot, k:k + 1], uact[:, :],
                                ALU.mult, ALU.add,
                                reads=["m:upad", "m:convw", "m:uact"],
                                writes=["m:uact"])
                        act(P, uab[:, :], uact[:, :], AF.Silu,
                            bias=convb[:, ot:ot + 1],
                            reads=["m:uact", "m:convb"], writes=["m:uab"])
                        P.dma("scq",
                              env.ua_h[ot * 128:(ot + 1) * 128,
                                       b * L + j:b * L + j + NB],
                              uab[:, :], reads=["m:uab"], writes=["m:uah"])
            if _ph < 3:
                continue
            with P.phase():
                dtr = P.sbuf("m_dtr", (64, L), BF16)
                Bm = P.sbuf("m_Bm", (16, L), BF16)
                Cm = P.sbuf("m_Cm", (16, L), BF16)
                dt_all = P.sbuf("m_dt", (128, 16, L), BF16)
                dtu_all = P.sbuf("m_dtu", (128, 16, L), BF16)
                P.pstacks.append(contextlib.ExitStack())
                ps_d = P.psum("m_ps_d", (64, NB), F32)
                ps_b = P.psum("m_ps_b", (16, NB), F32)
                ps_c = P.psum("m_ps_c", (16, NB), F32)
                ps_t = P.psum("m_ps_t", (128, NB), F32)
                uat = P.sbuf("m_uat", (128, 16, NB), BF16)
                wdt = P.sbuf("m_wdt", (128, 16, 64), BF16)
                wB = P.sbuf("m_wB", (128, 16, 16), BF16)
                wC = P.sbuf("m_wC", (128, 16, 16), BF16)
                P.dma("gpq", wdt[:, :, :], slab(env.hbm["l0_wdt"]), writes=["m:wdt"])
                P.dma("gpq", wB[:, :, :], slab(env.hbm["l0_wB"]), writes=["m:wB"])
                P.dma("gpq", wC[:, :, :], slab(env.hbm["l0_wC"]), writes=["m:wC"])
                for jc in range(4):
                    j = jc * NB
                    P.dma("syncq", uat[:, :, :],
                          env.ua_h.rearrange("(c p) q -> p c q", p=128)
                          [:, :, b * L + j:b * L + j + NB],
                          reads=["m:uat", "m:uah"], writes=["m:uat"])
                    mm_group(P, ps_d[:, :],
                             [(wdt[:, c, :], uat[:, c, :]) for c in range(16)],
                             reads=["m:wdt", "m:uat"], writes=["m:psd"])
                    vcopy(P, dtr[:, j:j + NB], ps_d[:, :], reads=["m:psd"],
                          writes=["m:dtr"])
                    mm_group(P, ps_b[:, :],
                             [(wB[:, c, :], uat[:, c, :]) for c in range(16)],
                             reads=["m:wB", "m:uat"], writes=["m:psb"])
                    vcopy(P, Bm[:, j:j + NB], ps_b[:, :], reads=["m:psb"],
                          writes=["m:Bm"])
                    mm_group(P, ps_c[:, :],
                             [(wC[:, c, :], uat[:, c, :]) for c in range(16)],
                             reads=["m:wC", "m:uat"], writes=["m:psc"])
                    vcopy(P, Cm[:, j:j + NB], ps_c[:, :], reads=["m:psc"],
                          writes=["m:Cm"])
                _m34 = os.environ.get("L0_M34", "c")
                wdtp = P.sbuf("m_wdtp", (64, 16, 128), BF16)
                t1 = P.sbuf("m_t1", (128, NB), F32)
                t2 = P.sbuf("m_t2", (128, NB), F32)
                P.dma("gpq", wdtp[:, :, :],
                      env.hbm["l0_dtw"].rearrange("c (t q) -> c t q", q=128),
                      writes=["m:wdtp"])
                for ot in range(16 if _m34 != "a" else 0):
                    for jc in range(4):
                        j = jc * NB
                        mmop(P, ps_t[:, :], wdtp[:, ot, :], dtr[:, j:j + NB],
                             reads=["m:wdtp", "m:dtr"], writes=["m:pst"])
                        _sp = os.environ.get("L0_SP", "full")
                        if _sp == "copy":
                            vcopy(P, dt_all[:, ot, j:j + NB], ps_t[:, :],
                                  reads=["m:pst"], writes=["m:dt"])
                            continue
                        act(P, t1[:, :], ps_t[:, :], AF.Abs,
                            bias=dtb[:, ot:ot + 1],
                            reads=["m:pst", "m:dtb"], writes=["m:t1"])
                        if _sp == "abs":
                            vcopy(P, dt_all[:, ot, j:j + NB], t1[:, :],
                                  reads=["m:t1"], writes=["m:dt"])
                            continue
                        act(P, t1[:, :], t1[:, :], AF.Exp, scale=-1.0,
                            reads=["m:t1"], writes=["m:t1b"])
                        if _sp == "exp":
                            vcopy(P, dt_all[:, ot, j:j + NB], t1[:, :],
                                  reads=["m:t1b"], writes=["m:dt"])
                            continue
                        act(P, t1[:, :], t1[:, :], AF.Ln, bias=env.cv("one"),
                            reads=["m:t1b", "c:cvec"], writes=["m:t1c"])
                        if _sp == "ln":
                            vcopy(P, dt_all[:, ot, j:j + NB], t1[:, :],
                                  reads=["m:t1c"], writes=["m:dt"])
                            continue
                        act(P, t2[:, :], ps_t[:, :], AF.Identity,
                            bias=dtb[:, ot:ot + 1],
                            reads=["m:pst", "m:dtb"], writes=["m:t2"])
                        P.op("vector",
                             lambda e, t=t2: e.tensor_scalar_max(t[:, :],
                                                                 t[:, :], 0.0),
                             reads=["m:t2"], writes=["m:t2"])
                        tt(P, dt_all[:, ot, j:j + NB], t1[:, :], t2[:, :],
                           ALU.add, reads=["m:t1c", "m:t2"], writes=["m:dt"])
                        if _m34 == "c":
                            P.dma("syncq", uat[:, 0, :],
                                  env.ua_h[ot * 128:(ot + 1) * 128,
                                           b * L + j:b * L + j + NB],
                                  reads=["m:uat", "m:uah"], writes=["m:uat"])
                            tt(P, dtu_all[:, ot, j:j + NB],
                               dt_all[:, ot, j:j + NB], uat[:, 0, :], ALU.mult,
                               reads=["m:dt", "m:uat"], writes=["m:dtu"])
                if _ph < 5:
                    P.pstacks.pop().close()
                    P.barrier()
                    continue
                P.pstacks.pop().close()
                P.barrier()
                ps_B4 = P.psum("m_ps_B4", (128, L), F32)
                ps_C4 = P.psum("m_ps_C4", (128, L), F32)
                dA = P.sbuf("m_dA", (128, L), F32)
                dBu = P.sbuf("m_dBu", (128, L), F32)
                hsc = P.sbuf("m_hsc", (128, L), F32)
                yac = P.sbuf("m_yac", (128, L), F32)
                ytm = P.sbuf("m_ytm", (128, L), F32)
                ygb = P.sbuf("m_ygb", (128, L), BF16)
                zt = P.sbuf("m_zt", (128, L), BF16)
                for ot in range(16):
                    memsetop(P, yac[:, :], 0.0, reads=["m:yac"],
                             writes=["m:yac"])
                    for n in range(16):
                        for jc in range(4):
                            j = jc * NB
                            mmop(P, ps_B4[:, j:j + NB],
                                 sel[:, n * 128:(n + 1) * 128],
                                 Bm[:, j:j + NB],
                                 reads=["m:sel", "m:Bm"], writes=["m:psB4"])
                            mmop(P, ps_C4[:, j:j + NB],
                                 sel[:, n * 128:(n + 1) * 128],
                                 Cm[:, j:j + NB],
                                 reads=["m:sel", "m:Cm"], writes=["m:psC4"])
                        act(P, dA[:, :], dt_all[:, ot, :], AF.Exp,
                            scale=As[:, ot, n:n + 1],
                            reads=["m:dt", "m:As"], writes=["m:dA"])
                        tt(P, dBu[:, :], dtu_all[:, ot, :], ps_B4[:, :],
                           ALU.mult, reads=["m:dtu", "m:psB4"], writes=["m:dBu"])
                        scanop(P, hsc[:, :], dA[:, :], dBu[:, :], 0.0,
                               reads=["m:dA", "m:dBu"], writes=["m:hsc"])
                        tt(P, ytm[:, :], hsc[:, :], ps_C4[:, :], ALU.mult,
                           reads=["m:hsc", "m:psC4"], writes=["m:ytm"])
                        tt(P, yac[:, :], yac[:, :], ytm[:, :], ALU.add,
                           reads=["m:yac", "m:ytm"], writes=["m:yac"])
                        if env.dbg and b == 0 and ot == 0 and n == 0:
                            P.dma("scq", env.scan_dbg[:, 0, :], dA[:, :],
                                  reads=["m:dA"], writes=["dbg:scan"])
                            P.dma("scq", env.scan_dbg[:, 1, :], dBu[:, :],
                                  reads=["m:dBu"], writes=["dbg:scan"])
                            P.dma("scq", env.scan_dbg[:, 2, :], hsc[:, :],
                                  reads=["m:hsc"], writes=["dbg:scan"])
                            P.dma("scq", env.scan_dbg[:, 3, :], yac[:, :],
                                  reads=["m:yac"], writes=["dbg:scan"])
                            vcopy(P, dA[:, :], ps_B4[:, :], reads=["m:psB4"],
                                  writes=["m:dA"])
                            P.dma("scq", env.scan_dbg[:, 4, :], dA[:, :],
                                  reads=["m:dA"], writes=["dbg:scan"])
                            vcopy(P, dA[:, :], dt_all[:, 0, :], reads=["m:dt"],
                                  writes=["m:dA"])
                            P.dma("scq", env.scan_dbg[:, 5, :], dA[:, :],
                                  reads=["m:dA"], writes=["dbg:scan"])
                    P.dma("syncq", zt[:, :],
                          env.ua_h[ot * 128:(ot + 1) * 128, tok],
                          reads=["m:zt", "m:uah"], writes=["m:zt"])
                    if env.dbg and b == 0 and ot == 0:
                        P.dma("scq", env.scan_dbg[:, 3, :], yac[:, :],
                              reads=["m:yac"], writes=["dbg:scan"])
                        P.dma("gpq", env.scan_dbg[:, 4, :], zt[:, :],
                              reads=["m:zt"], writes=["dbg:scan"])
                    stt(P, yac[:, :], zt[:, :], Dc[:, ot:ot + 1], yac[:, :],
                        ALU.mult, ALU.add, reads=["m:zt", "m:Dc", "m:yac"],
                        writes=["m:yac"])
                    if env.dbg and b == 0 and ot == 0:
                        P.dma("scq", env.scan_dbg[:, 5, :], yac[:, :],
                              reads=["m:yac"], writes=["dbg:scan"])
                    P.dma("syncq", zt[:, :],
                          env.z_h[ot * 128:(ot + 1) * 128, tok],
                          reads=["m:zt", "m:zh"], writes=["m:zt"])
                    act(P, ytm[:, :], zt[:, :], AF.Silu, reads=["m:zt"],
                        writes=["m:zsil"])
                    tt(P, ygb[:, :], yac[:, :], ytm[:, :], ALU.mult,
                       reads=["m:yac", "m:zsil"], writes=["m:ygb"])
                    P.dma("scq", env.yg_h[ot * 128:(ot + 1) * 128, tok],
                          ygb[:, :], reads=["m:ygb"], writes=["m:ygh"])
            if _ph < 6:
                continue
            with P.phase():
                ps2 = P.psum("m6_ps", (128, 1024), F32)
                ygc = P.sbuf("m6_ygc", (128, 16, NB), BF16)
                wsl = P.sbuf("m6_wsl", (128, 16, 128), BF16)
                xnt = P.sbuf("m6_xnt", (128, NB), BF16)
                hst = P.sbuf("m6_hst", (128, NB), F32)
                for jc in range(4):
                    j = jc * NB
                    P.dma("syncq", ygc[:, :, :],
                          env.yg_h.rearrange("(c p) q -> p c q", p=128)
                          [:, :, b * L + j:b * L + j + NB],
                          reads=["m6:ygc", "m:ygh"], writes=["m6:ygc"])
                    for ot in range(8):
                        P.dma("gpq", wsl[:, :, :],
                              slab(env.hbm["l0_outproj"])
                              [:, :, ot * 128:(ot + 1) * 128],
                              reads=["m6:wsl"], writes=["m6:wsl"])
                        pslc = ps2[:, (ot % 2) * NB:(ot % 2) * NB + NB]
                        pk = f"m6:ps{ot % 2}"
                        mm_group(P, pslc,
                                 [(wsl[:, c, :], ygc[:, c, :])
                                  for c in range(16)],
                                 reads=["m6:wsl", "m6:ygc"], writes=[pk])
                        P.dma("syncq", xnt[:, :],
                              env.xn_h[ot * 128:(ot + 1) * 128,
                                       b * L + j:b * L + j + NB],
                              reads=["m6:xnt", f"xn:{ot}"], writes=["m6:xnt"])
                        tt(P, hst[:, :], pslc, xnt[:, :], ALU.add,
                           reads=[pk, "m6:xnt"], writes=["m6:hst"])
                        P.dma("gpq",
                              env.h_h[ot * 128:(ot + 1) * 128,
                                      b * L + j:b * L + j + NB],
                              hst[:, :], reads=["m6:hst"], writes=["m:hh"])
        _ndm = int(os.environ.get("DUMMY_AT_M7", "0"))
        if _ndm:
            with P.phase():
                db2 = P.sbuf("dummy2", (128, NB), F32)
                for _i in range(_ndm):
                    _ii, _jj = _i % 8, (_i // 8) % 8 * NB
                    P.dma("syncq", db2[:, :],
                          env.h_h[_ii * 128:(_ii + 1) * 128, _jj:_jj + NB],
                          reads=["dm2:b", "m:hh"], writes=["dm2:b"])
        if _ph < 7:
            return
        with P.phase():
            _mk = int(os.environ.get("L0_M7K", "12"))
            _allocs = [
                lambda: P.psum("m7_ps", (128, 1024), F32),
                lambda: P.psum("m7_ps_ms", (1, NB), F32),
                lambda: P.sbuf("m7_hti", (128, NB), F32),
                lambda: P.sbuf("m7_sq", (128, NB), F32),
                lambda: P.sbuf("m7_rows", (1, T), F32),
                lambda: P.sbuf("m7_rstdb", (128, NB), F32),
                lambda: P.sbuf("m7_hr", (128, 8, NB), BF16),
                lambda: P.sbuf("m7_mm", (128, 32, NB), BF16),
                lambda: P.sbuf("m7_p2", (128, NB), F32),
                lambda: P.sbuf("m7_wsl", (128, 8, 128), BF16),
                lambda: P.sbuf("m7_wsl3", (128, 32, 128), BF16),
                lambda: P.sbuf("m7_sst", (128, NB), F32),
            ]
            _got = [f() for f in _allocs[:_mk]]
            while len(_got) < 12:
                _got.append(None)
            (ps2, ps_ms, hti, sq, rows, rstdb, hr, mmb, p2b, wsl, wsl3,
             sst) = _got
            _m7s = os.environ.get("L0_M7S", "all")
            if _m7s == "none":
                return
            if _m7s == "nostats":
                memsetop(P, rows[:, :], 1.0, reads=["m7:rstd"],
                         writes=["m7:rstd"])
            for jc in range(0 if _m7s == "nostats" else T // NB):
                j = jc * NB
                for i in range(8):
                    _src = (env.xn_h if os.environ.get("L0_M7X") == "1"
                            else env.h_h)
                    P.dma("syncq", hti[:, :] if _src is env.h_h else hti[:, :].bitcast(BF16)[:, 0:NB],
                          _src[i * 128:(i + 1) * 128, j:j + NB],
                          reads=["m7:hti", "m:hh"], writes=["m7:hti"])
                    if _m7s == "loads":
                        continue
                    act(P, sq[:, :], hti[:, :], AF.Square, reads=["m7:hti"],
                        writes=["m7:sq"])
                    mmop(P, ps_ms[:, :], env.ones_col[:, :], sq[:, :],
                         reads=["m7:sq", "c:ones_col"], writes=["m7:psms"],
                         start=(i == 0), stop=(i == 7))
                if _m7s != "loads":
                    act(P, rows[:, j:j + NB], ps_ms[:, :], AF.Ln, scale=1.0 / D,
                        bias=env.cv("eps5", 1), reads=["m7:psms", "c:cvec"],
                        writes=["m7:lnr"])
                    act(P, rows[:, j:j + NB], rows[:, j:j + NB], AF.Exp,
                        scale=-0.5, reads=["m7:lnr"], writes=["m7:rstd"])
            _m7 = int(os.environ.get("L0_M7", "9"))
            for jc in range(T // NB if _m7 >= 2 else 0):
                j = jc * NB
                bcast_row(P, env, rows[:, j:j + NB], rstdb, NB, ["m7:rstd"],
                          "m7:rstdb", ps2[:, 0:NB], "m7:ps0")
                for i in range(8):
                    P.dma("syncq", hti[:, :], env.h_h[i * 128:(i + 1) * 128,
                                                      j:j + NB],
                          reads=["m7:hti", "m:hh"], writes=["m7:hti"])
                    stt(P, hr[:, i, :], hti[:, :], n2c[:, i:i + 1], rstdb[:, :],
                        ALU.mult, ALU.mult,
                        reads=["m7:hti", "m:n2c", "m7:rstdb"], writes=["m7:hr"])
                for ot in range(32 if _m7 >= 3 else 0):
                    P.dma("gpq", wsl[:, :, :],
                          slab(env.hbm["l0_w1T"])[:, :, ot * 128:(ot + 1) * 128],
                          reads=["m7:wsl"], writes=["m7:wsl"])
                    pslc = ps2[:, 0:NB]
                    mm_group(P, pslc,
                             [(wsl[:, c, :], hr[:, c, :]) for c in range(8)],
                             reads=["m7:wsl", "m7:hr"], writes=["m7:ps0"])
                    act(P, mmb[:, ot, :], pslc, AF.Silu, reads=["m7:ps0"],
                        writes=["m7:mm"])
                    P.dma("gpq", wsl[:, :, :],
                          slab(env.hbm["l0_w2T"])[:, :, ot * 128:(ot + 1) * 128],
                          reads=["m7:wsl"], writes=["m7:wsl"])
                    pslc = ps2[:, NB:2 * NB]
                    mm_group(P, pslc,
                             [(wsl[:, c, :], hr[:, c, :]) for c in range(8)],
                             reads=["m7:wsl", "m7:hr"], writes=["m7:ps1"])
                    vcopy(P, p2b[:, :], pslc, reads=["m7:ps1"], writes=["m7:p2"])
                    tt(P, mmb[:, ot, :], mmb[:, ot, :], p2b[:, :], ALU.mult,
                       reads=["m7:mm", "m7:p2"], writes=["m7:mm"])
                for ot in range(8 if _m7 >= 4 else 0):
                    P.dma("gpq", wsl3[:, :, :],
                          slab(env.hbm["l0_w3T"])[:, :, ot * 128:(ot + 1) * 128],
                          reads=["m7:wsl3"], writes=["m7:wsl3"])
                    pslc = ps2[:, (ot % 2) * NB:(ot % 2) * NB + NB]
                    pk = f"m7:ps{ot % 2}"
                    mm_group(P, pslc,
                             [(wsl3[:, c, :], mmb[:, c, :]) for c in range(32)],
                             reads=["m7:wsl3", "m7:mm"], writes=[pk])
                    P.dma("syncq", hti[:, :],
                          env.h_h[ot * 128:(ot + 1) * 128, j:j + NB],
                          reads=["m7:hti", "m:hh"], writes=["m7:hti"])
                    tt(P, sst[:, :], pslc, hti[:, :], ALU.add,
                       reads=[pk, "m7:hti"], writes=["m7:sst"])
                    _starget = (env.sx_h if os.environ.get("L0_SX") == "1"
                                else env.s_h)
                    P.dma("scq", _starget[ot * 128:(ot + 1) * 128, j:j + NB],
                          sst[:, :], reads=["m7:sst"], writes=["m7:sh"])




def emit_L0_ffn(P, env):
    SCA = 1.0 / 32.0
    with P.phase():
        with P.phase():
            cb = P.sbuf("f_castb", (128, 32, NB), BF16)
            for wname in ("l0q_w1c", "l0k_w1c", "l0v_w1c", "l0f_w1c"):
                for mc in range(8):
                    P.dma("gpq", cb[:, 0:8, :],
                          slab(env.hbm[wname])[:, :, mc * NB:(mc + 1) * NB],
                          reads=["f:cb"], writes=["f:cb"])
                    P.dma("scq",
                          env.hbm[wname + "_b"].rearrange("(c p) q -> p c q",
                                                          p=128)
                          [:, :, mc * NB:(mc + 1) * NB],
                          cb[:, 0:8, :], reads=["f:cb"], writes=[f"f:{wname}b"])
            for wname in ("l0q_w2c", "l0k_w2c", "l0v_w2c", "l0f_w2c"):
                for mc in range(2):
                    P.dma("gpq", cb[:, :, :],
                          slab(env.hbm[wname])[:, :, mc * NB:(mc + 1) * NB],
                          reads=["f:cb"], writes=["f:cb"])
                    P.dma("scq",
                          env.hbm[wname + "_b"].rearrange("(c p) q -> p c q",
                                                          p=128)
                          [:, :, mc * NB:(mc + 1) * NB],
                          cb[:, :, :], reads=["f:cb"], writes=[f"f:{wname}b"])
        expsel = P.sbuf("f_expsel", (128, 32, 128), BF16)
        P.dma("syncq", expsel[:, :, :], env.hbm["l0_expsel"],
              writes=["f:expsel"])

        for b in range(B_LOC):
            tok = slice(b * L, (b + 1) * L)
            with P.phase():
                yb = P.sbuf("f_yb", (128, 8, L), BF16)
                with P.phase():
                    xfb = P.sbuf("f_xfb", (128, 8, L), BF16)
                    with P.phase():
                        ps1 = P.psum("f_ps1", (128, 1024), F32)
                        ps_mu = P.psum("f_ps_mu", (1, NB), F32)
                        ps_ms = P.psum("f_ps_ms", (1, NB), F32)
                        xnt = P.sbuf("f_xnt", (128, NB), F32)
                        sq = P.sbuf("f_sq", (128, NB), F32)
                        rows = P.sbuf("f_rows", (1, 2 * L), F32)
                        t1b = P.sbuf("f_t1r", (1, NB), F32)
                        mur = rows[:, 0:L]; rde = rows[:, L:2 * L]
                        for jc in range(4):
                            j = jc * NB
                            for i in range(8):
                                P.dma("syncq", xnt[:, :],
                                      env.xn32_h[i * 128:(i + 1) * 128,
                                                 b * L + j:b * L + j + NB],
                                      reads=["f:xnt", f"xn32:{i}"],
                                      writes=["f:xnt"])
                                act(P, sq[:, :], xnt[:, :], AF.Square,
                                    reads=["f:xnt"], writes=["f:sq"])
                                mmop(P, ps_mu[:, :], env.ones_col[:, :],
                                     xnt[:, :],
                                     reads=["f:xnt", "c:ones_col"],
                                     writes=["f:psmu"], start=(i == 0),
                                     stop=(i == 7))
                                mmop(P, ps_ms[:, :], env.ones_col[:, :], sq[:, :],
                                     reads=["f:sq", "c:ones_col"],
                                     writes=["f:psms"], start=(i == 0),
                                     stop=(i == 7))
                            jsl = slice(j, j + NB)
                            ts(P, mur[:, jsl], ps_mu[:, :], 1.0 / D, ALU.mult,
                               reads=["f:psmu"], writes=["f:mu"])
                            ts(P, t1b[:, :], ps_ms[:, :], 1.0 / 1023.0, ALU.mult,
                               reads=["f:psms"], writes=["f:t1r"])
                            tt(P, rde[:, jsl], mur[:, jsl], mur[:, jsl], ALU.mult,
                               reads=["f:mu"], writes=["f:musq"])
                            stt(P, rde[:, jsl], rde[:, jsl], 1024.0 / 1023.0,
                                t1b[:, :], ALU.mult, ALU.subtract,
                                reads=["f:musq", "f:t1r"], writes=["f:var"])
                            ts(P, rde[:, jsl], rde[:, jsl], -1.0, ALU.mult,
                               reads=["f:var"], writes=["f:var2"])
                            act(P, rde[:, jsl], rde[:, jsl], AF.Sqrt,
                                reads=["f:var2"], writes=["f:sd"])
                            act(P, rde[:, jsl], rde[:, jsl], AF.Ln,
                                bias=env.cv("eps5", 1),
                                reads=["f:sd", "c:cvec"], writes=["f:lnd"])
                            act(P, rde[:, jsl], rde[:, jsl], AF.Exp, scale=-0.5,
                                reads=["f:lnd"], writes=["f:rde"])
                            tt(P, mur[:, jsl], mur[:, jsl], rde[:, jsl], ALU.mult,
                               reads=["f:mu", "f:rde"], writes=["f:murd"])
                        rdb = P.sbuf("f_rdb", (128, L), F32)
                        mrb = P.sbuf("f_mrb", (128, L), F32)
                        xf32t = P.sbuf("f_xf32t", (128, NB), F32)
                        bcast_row(P, env, rde, rdb, L, ["f:rde"], "f:rdb",
                                  ps1[:, 0:NB], "f:ps0")
                        bcast_row(P, env, mur, mrb, L, ["f:murd"], "f:mrb",
                                  ps1[:, 0:NB], "f:ps0")
                        for i in range(8):
                            for jc in range(4):
                                j = jc * NB
                                P.dma("syncq", xnt[:, :],
                                      env.xn32_h[i * 128:(i + 1) * 128,
                                                 b * L + j:b * L + j + NB],
                                      reads=["f:xnt", f"xn32:{i}"],
                                      writes=["f:xnt"])
                                tt(P, xf32t[:, :], xnt[:, :],
                                   rdb[:, j:j + NB], ALU.mult,
                                   reads=["f:xnt", "f:rdb"], writes=["f:xf32"])
                                tt(P, xf32t[:, :], xf32t[:, :],
                                   mrb[:, j:j + NB], ALU.subtract,
                                   reads=["f:xf32", "f:mrb"], writes=["f:xf32"])
                                vcopy(P, xfb[:, i, j:j + NB], xf32t[:, :],
                                      reads=["f:xf32"], writes=["f:xfb"])
                                P.dma("scq",
                                      env.xf32_h[i * 128:(i + 1) * 128, j:j + NB],
                                      xf32t[:, :], reads=["f:xf32"],
                                      writes=["f:xf32h"])
                    def fff(pref, inb, ink, outfn, actfn,
                            fp32g=False):
                        with P.phase():
                            ps_r = P.psum("fr_ps", (128, NB), F32)
                            ps_e = P.psum("fr_ps_e", (128, NB), F32)
                            ps_g = P.psum("fr_ps_g", (128, 2 * NB), F32)
                            ps_pt = P.psum("fr_ps_pt", (128, 1024), BF16)
                            PT = P.sbuf("fr_PT", (128, 2, L), BF16)
                            PT32 = P.sbuf("fr_PT32", (128, 2, L), F32)
                            b1c = P.sbuf("fr_b1c", (128, 32), F32)
                            P.pstacks.append(contextlib.ExitStack())
                            nwt = P.sbuf("fr_nwt", (128, 8, 256), F32)
                            xft = P.sbuf("fr_xft", (128, 8, 128), F32)
                            nbr = P.sbuf("fr_nbr", (1, 256), F32)
                            nbb = P.sbuf("fr_nbb", (128, 256), F32)
                            dT = P.sbuf("fr_dT", (128, 256), BF16)
                            dbT = P.sbuf("fr_dbT", (128, 256), BF16)
                            PA = P.sbuf("fr_PA", (128, 256), BF16)
                            PB = P.sbuf("fr_PB", (128, 256), BF16)
                            P.dma("syncq", nwt[:, :, :],
                                  slab(env.hbm[pref + "nwT"]), writes=["fr:nwt"])
                            P.dma("syncq", nbr[:, :], env.hbm[pref + "nb"],
                                  writes=["fr:nbr"])
                            P.dma("syncq", b1c[:, :], env.hbm[pref + "b1c"],
                                  writes=["fr:b1c"])
                            ts(P, nbr[:, :], nbr[:, :], -1.0, ALU.mult,
                               reads=["fr:nbr"], writes=["fr:nbr"])
                            bcast_row(P, env, nbr[:, :], nbb, 256, ["fr:nbr"],
                                      "fr:nbb", ps_r[:, 0:256], "fr:psr")
                            src32 = env.xf32_h if ink == "f:xfb" else env.y32_h
                            sk32 = "f:xf32h" if ink == "f:xfb" else "f:y32h"
                            for tt_ in range(16):
                                tsl = slice(tt_ * 128, (tt_ + 1) * 128)
                                P.dma("syncq", xft[:, :, :],
                                      src32.rearrange("(c p) q -> p c q", p=128)
                                      [:, :, tsl],
                                      reads=["fr:xft", sk32], writes=["fr:xft"])
                                mm_group(P, ps_r[:, 0:256],
                                         [(xft[:, c, :], nwt[:, c, :])
                                          for c in range(8)],
                                         reads=["fr:xft", "fr:nwt"],
                                         writes=["fr:psr"])
                                tt(P, dT[:, :], ps_r[:, 0:256], nbb[:, :],
                                   ALU.is_gt, reads=["fr:psr", "fr:nbb"],
                                   writes=["fr:dT"])
                                tt(P, dbT[:, :], ps_r[:, 0:256], nbb[:, :],
                                   ALU.is_le, reads=["fr:psr", "fr:nbb"],
                                   writes=["fr:dbT"])
                                vcopy(P, PA[:, 0:1], dbT[:, 0:1],
                                      reads=["fr:dbT"], writes=["fr:PA"])
                                vcopy(P, PA[:, 1:2], dT[:, 0:1],
                                      reads=["fr:dT"], writes=["fr:PA"])
                                cur, nxt = PA, PB
                                ck, nk = "fr:PA", "fr:PB"
                                for lv in range(1, 8):
                                    w_ = 1 << lv
                                    off = w_ - 1
                                    o3 = nxt[:, 0:2 * w_].rearrange(
                                        "p (a two) -> p a two", two=2)
                                    tt(P, o3[:, :, 0], cur[:, 0:w_],
                                       dbT[:, off:off + w_], ALU.mult,
                                       reads=[ck, "fr:dbT"], writes=[nk])
                                    tt(P, o3[:, :, 1], cur[:, 0:w_],
                                       dT[:, off:off + w_], ALU.mult,
                                       reads=[ck, "fr:dT"], writes=[nk])
                                    cur, nxt = nxt, cur
                                    ck, nk = nk, ck
                                for lt in range(2):
                                    trop(P, ps_pt[:, 0:128],
                                         cur[:, lt * 128:(lt + 1) * 128],
                                         env.ident_b[:, :],
                                         reads=[ck, "c:ident_b"],
                                         writes=["fr:pspt"])
                                    vcopy(P, PT[:, lt, tsl], ps_pt[:, 0:128],
                                          reads=["fr:pspt"], writes=["fr:PT"])
                                    if fp32g:
                                        vcopy(P, PT32[:, lt, tsl],
                                              ps_pt[:, 0:128],
                                              reads=["fr:pspt"],
                                              writes=["fr:PT32"])
                            P.pstacks.pop().close()
                            P.barrier()
                            if fp32g:
                                xfsl = P.sbuf("fr_xfsl", (128, 8, NB), F32)
                                wsl32 = P.sbuf("fr_wsl32", (128, 8, 128), F32)
                                hbuf32 = P.sbuf("fr_hb32", (128, NB), F32)
                                w2s32 = P.sbuf("fr_w2s32", (128, 16, 128), F32)
                                b2s32 = P.sbuf("fr_b2s32", (128, 2, 128), F32)
                                hch32 = P.sbuf("fr_hch32", (128, 16, NB), F32)
                                for jc in range(4):
                                    j = jc * NB
                                    P.dma("syncq", xfsl[:, :, :],
                                          src32.rearrange(
                                              "(c p) q -> p c q", p=128)
                                          [:, :, j:j + NB],
                                          reads=["fr:xfsl", sk32],
                                          writes=["fr:xfsl"])
                                    for ot in range(32):
                                        P.dma("syncq", wsl32[:, :, :],
                                              env.hbm[pref + "w1c"]
                                              .rearrange("(c p) q -> p c q",
                                                         p=128)
                                              [:, :, ot * 128:(ot + 1) * 128],
                                              reads=["fr:wsl"],
                                              writes=["fr:wsl"])
                                        lt = ot // 16
                                        pslc = ps_g[:, (ot % 2) * NB:
                                                    (ot % 2) * NB + NB]
                                        pk = f"fr:psg{ot % 2}"
                                        mm_group(P, pslc,
                                                 [(wsl32[:, c, :],
                                                   xfsl[:, c, :])
                                                  for c in range(8)],
                                                 reads=["fr:wsl", "fr:xfsl"],
                                                 writes=[pk])
                                        act(P, hbuf32[:, :], pslc, actfn,
                                            bias=b1c[:, ot:ot + 1],
                                            reads=[pk, "fr:b1c"],
                                            writes=["fr:hbuf"])
                                        mmop(P, ps_e[:, :], expsel[:, ot, :],
                                             PT[:, lt, j:j + NB],
                                             reads=["f:expsel", "fr:PT"],
                                             writes=["fr:pse"])
                                        tt(P, hbuf32[:, :], hbuf32[:, :],
                                           ps_e[:, :], ALU.mult,
                                           reads=["fr:hbuf", "fr:pse"],
                                           writes=["fr:hbuf"])
                                        P.dma("scq",
                                              env.hf32_h[ot * 128:
                                                         (ot + 1) * 128,
                                                         j:j + NB],
                                              hbuf32[:, :],
                                              reads=["fr:hbuf"],
                                              writes=["fr:hfh"])
                                for jc in range(4):
                                    j = jc * NB
                                    for dt8 in range(8):
                                        P.dma("gpq", b2s32[:, :, :],
                                              slab(env.hbm[pref + "b2"])
                                              [:, :, dt8 * 128:
                                               (dt8 + 1) * 128],
                                              reads=["fr:b2s"],
                                              writes=["fr:b2s"])
                                        pslc = ps_g[:, (dt8 % 2) * NB:
                                                    (dt8 % 2) * NB + NB]
                                        pk = f"fr:psg{dt8 % 2}"
                                        for hh in range(2):
                                            P.dma("syncq", w2s32[:, :, :],
                                                  env.hbm[pref + "w2c"]
                                                  .rearrange(
                                                      "(c p) q -> p c q",
                                                      p=128)
                                                  [:, hh * 16:(hh + 1) * 16,
                                                   dt8 * 128:(dt8 + 1) * 128],
                                                  reads=["fr:w2s"],
                                                  writes=["fr:w2s"])
                                            P.dma("syncq", hch32[:, :, :],
                                                  env.hf32_h.rearrange(
                                                      "(c p) q -> p c q",
                                                      p=128)
                                                  [:, hh * 16:(hh + 1) * 16,
                                                   j:j + NB],
                                                  reads=["fr:hch", "fr:hfh"],
                                                  writes=["fr:hch"])
                                            for ci in range(16):
                                                mmop(P, pslc,
                                                     w2s32[:, ci, :],
                                                     hch32[:, ci, :],
                                                     reads=["fr:w2s",
                                                            "fr:hch"],
                                                     writes=[pk],
                                                     start=(hh == 0 and
                                                            ci == 0),
                                                     stop=False)
                                        mmop(P, pslc, b2s32[:, 0, :],
                                             PT32[:, 0, j:j + NB],
                                             reads=["fr:b2s", "fr:PT32"],
                                             writes=[pk],
                                             start=False, stop=False)
                                        mmop(P, pslc, b2s32[:, 1, :],
                                             PT32[:, 1, j:j + NB],
                                             reads=["fr:b2s", "fr:PT32"],
                                             writes=[pk],
                                             start=False, stop=True)
                                        outfn(dt8, j, pslc, pk)
                                return
                            wsl = P.sbuf("fr_wsl", (128, 8, 128), BF16)
                            w2s = P.sbuf("fr_w2s", (128, 16, 128), BF16)
                            b2s = P.sbuf("fr_b2s", (128, 2, 128), BF16)
                            hbuf = P.sbuf("fr_hb", (128, NB), BF16)
                            hch = P.sbuf("fr_hch", (128, 16, NB), BF16)
                            for ot in range(32):
                                P.dma("syncq", wsl[:, :, :],
                                      env.hbm[pref + "w1c_b"]
                                      .rearrange("(c p) q -> p c q", p=128)
                                      [:, :, ot * 128:(ot + 1) * 128],
                                      reads=["fr:wsl"], writes=["fr:wsl"])
                                lt = ot // 16
                                for jc in range(4):
                                    j = jc * NB
                                    pslc = ps_g[:, (jc % 2) * NB:(jc % 2) * NB + NB]
                                    pk = f"fr:psg{jc % 2}"
                                    mm_group(P, pslc,
                                             [(wsl[:, c, :], inb[:, c, j:j + NB])
                                              for c in range(8)],
                                             reads=["fr:wsl", ink], writes=[pk])
                                    act(P, hbuf[:, :], pslc, actfn,
                                        bias=b1c[:, ot:ot + 1],
                                        reads=[pk, "fr:b1c"], writes=["fr:hbuf"])
                                    mmop(P, ps_e[:, :], expsel[:, ot, :],
                                         PT[:, lt, j:j + NB],
                                         reads=["f:expsel", "fr:PT"],
                                         writes=["fr:pse"])
                                    tt(P, hbuf[:, :], hbuf[:, :], ps_e[:, :],
                                       ALU.mult, reads=["fr:hbuf", "fr:pse"],
                                       writes=["fr:hbuf"])
                                    P.dma("scq",
                                          env.hf_h[ot * 128:(ot + 1) * 128,
                                                   j:j + NB],
                                          hbuf[:, :], reads=["fr:hbuf"],
                                          writes=["fr:hfh"])
                            for jc in range(4):
                                j = jc * NB
                                for dt8 in range(8):
                                    P.dma("gpq", b2s[:, :, :],
                                          slab(env.hbm[pref + "b2"])
                                          [:, :, dt8 * 128:(dt8 + 1) * 128],
                                          reads=["fr:b2s"], writes=["fr:b2s"])
                                    pslc = ps_g[:, (dt8 % 2) * NB:(dt8 % 2) * NB + NB]
                                    pk = f"fr:psg{dt8 % 2}"
                                    for hh in range(2):
                                        P.dma("syncq", w2s[:, :, :],
                                              env.hbm[pref + "w2c_b"]
                                              .rearrange("(c p) q -> p c q", p=128)
                                              [:, hh * 16:(hh + 1) * 16,
                                               dt8 * 128:(dt8 + 1) * 128],
                                              reads=["fr:w2s"], writes=["fr:w2s"])
                                        P.dma("syncq", hch[:, :, :],
                                              env.hf_h.rearrange(
                                                  "(c p) q -> p c q", p=128)
                                              [:, hh * 16:(hh + 1) * 16, j:j + NB],
                                              reads=["fr:hch", "fr:hfh"],
                                              writes=["fr:hch"])
                                        for ci in range(16):
                                            mmop(P, pslc,
                                                 w2s[:, ci, :],
                                                 hch[:, ci, :],
                                                 reads=["fr:w2s", "fr:hch"],
                                                 writes=[pk],
                                                 start=(hh == 0 and ci == 0),
                                                 stop=False)
                                    mmop(P, pslc, b2s[:, 0, :], PT[:, 0, j:j + NB],
                                         reads=["fr:b2s", "fr:PT"], writes=[pk],
                                         start=False, stop=False)
                                    mmop(P, pslc, b2s[:, 1, :], PT[:, 1, j:j + NB],
                                         reads=["fr:b2s", "fr:PT"], writes=[pk],
                                         start=False, stop=True)
                                    outfn(dt8, j, pslc, pk)

                    def out_to(dst, key):
                        def f(dt8, j, pslc, pk):
                            act(P, dst[:, dt8, j:j + NB], pslc, AF.Identity,
                                reads=[pk], writes=[key])
                        return f

                    def out32(hdst, key):
                        def f(dt8, j, pslc, pk):
                            act(P, st32[:, :], pslc, AF.Identity,
                                reads=[pk], writes=["f:st32"])
                            P.dma("scq", hdst[dt8 * 128:(dt8 + 1) * 128, j:j + NB],
                                  st32[:, :], reads=["f:st32"], writes=[key])
                        return f

                    with P.phase():
                        st32 = P.sbuf("f_st32", (128, NB), F32)
                        fff("l0v_", xfb, "f:xfb", out32(env.v32_h, "f:v32h"),
                            AF.Relu, fp32g=True)
                        fff("l0q_", xfb, "f:xfb", out32(env.q32_h, "f:q32h"),
                            AF.Relu, fp32g=True)
                        fff("l0k_", xfb, "f:xfb", out32(env.k32_h, "f:k32h"),
                            AF.Relu, fp32g=True)
                with P.phase():
                    ps_sc = P.psum("fa_ps_sc", (128, 2048), F32)
                    ps_pt = P.psum("fa_ps_pt", (128, 1024), F32)
                    ps_o = P.psum("fa_ps_o", (128, NB), F32)
                    k32 = P.sbuf("fa_k32", (128, 8, L), F32)
                    v32tm = P.sbuf("fa_v32tm", (128, 16, 1024), F32)
                    q32t = P.sbuf("fa_q32t", (128, 8, 128), F32)
                    vsl = P.sbuf("fa_vsl", (128, 128), F32)
                    probs = P.sbuf("fa_probs", (128, 2048), F32)
                    probsT = P.sbuf("fa_probsT", (128, 16, 128), F32)
                    cols = P.sbuf("fa_cols", (128, 4), F32)
                    y32st = P.sbuf("fa_y32", (128, 128), F32)
                    xf32a = P.sbuf("fa_xf32a", (128, 128), F32)
                    negmax = cols[:, 0:1]; den = cols[:, 1:2]; rec = cols[:, 2:3]
                    for i in range(8):
                        P.dma("syncq", k32[:, i, :],
                              env.k32_h[i * 128:(i + 1) * 128, :],
                              reads=["fa:k32", "f:k32h"], writes=["fa:k32"])
                    for i in range(8):
                        for tt_ in range(16):
                            P.dma("syncq", vsl[:, :],
                                  env.v32_h[i * 128:(i + 1) * 128,
                                            tt_ * 128:(tt_ + 1) * 128],
                                  reads=["fa:vsl", "f:v32h"], writes=["fa:vsl"])
                            sl8 = slice((tt_ % 8) * 128, (tt_ % 8) * 128 + 128)
                            trop(P, ps_pt[:, sl8], vsl[:, :],
                                 env.ident_f[:, :],
                                 reads=["fa:vsl", "c:ident_f"],
                                 writes=["fa:pspt"])
                            vcopy(P, v32tm[:, tt_, i * 128:(i + 1) * 128],
                                  ps_pt[:, sl8],
                                  reads=["fa:pspt"], writes=["fa:v32tm"])
                    for qt in range(16):
                        qsl = slice(qt * 128, (qt + 1) * 128)
                        P.dma("syncq", q32t[:, :, :],
                              env.q32_h.rearrange("(c p) q -> p c q", p=128)
                              [:, :, qsl],
                              reads=["fa:q32t", "f:q32h"], writes=["fa:q32t"])
                        for ch in range(4):
                            ksl = slice(ch * NB, (ch + 1) * NB)
                            for c in range(8):
                                mmop(P, ps_sc[:, ksl], q32t[:, c, :],
                                     k32[:, c, ksl],
                                     reads=["fa:q32t", "fa:k32"],
                                     writes=[f"fa:ps{ch}"],
                                     start=(c == 0), stop=(c == 7))
                        redop(P, negmax, ps_sc[:, :], ALU.max,
                              reads=[f"fa:ps{c}" for c in range(4)],
                              writes=["fa:negmax"], negate=True)
                        ts(P, negmax, negmax, SCA, ALU.mult,
                           reads=["fa:negmax"], writes=["fa:negmax"])
                        act(P, probs[:, :], ps_sc[:, :], AF.Exp, scale=SCA,
                            bias=negmax, accum_out=den,
                            reads=[f"fa:ps{c}" for c in range(4)] + ["fa:negmax"],
                            writes=["fa:probs", "fa:den"])
                        recipop(P, rec, den, reads=["fa:den"], writes=["fa:rec"])
                        ts(P, probs[:, :], probs[:, :], rec, ALU.mult,
                           reads=["fa:probs", "fa:rec"], writes=["fa:probs"])
                        for half in range(2):
                            for kt8 in range(8):
                                kt = half * 8 + kt8
                                trop(P, ps_pt[:, kt8 * 128:(kt8 + 1) * 128],
                                     probs[:, kt * 128:(kt + 1) * 128],
                                     env.ident_f[:, :],
                                     reads=["fa:probs", "c:ident_f"],
                                     writes=["fa:pspt"])
                            vcopy(P, probsT[:, half * 8:(half + 1) * 8, :]
                                  .rearrange("p a q -> p (a q)"),
                                  ps_pt[:, 0:1024], reads=["fa:pspt"],
                                  writes=["fa:probsT"])
                        for i in range(8):
                            for kt in range(16):
                                mmop(P, ps_o[:, 0:128],
                                     v32tm[:, kt, i * 128:(i + 1) * 128],
                                     probsT[:, kt, :],
                                     reads=["fa:v32tm", "fa:probsT"],
                                     writes=["fa:pso"], start=(kt == 0),
                                     stop=(kt == 15))
                            P.dma("syncq", xf32a[:, :],
                                  env.xf32_h[i * 128:(i + 1) * 128, qsl],
                                  reads=["fa:xf32a", "f:xf32h"],
                                  writes=["fa:xf32a"])
                            tt(P, y32st[:, :], ps_o[:, 0:128],
                               xf32a[:, :], ALU.add,
                               reads=["fa:pso", "fa:xf32a"], writes=["fa:y32"])
                            vcopy(P, yb[:, i, qsl], y32st[:, :],
                                  reads=["fa:y32"], writes=["f:yb"])
                            P.dma("scq",
                                  env.y32_h[i * 128:(i + 1) * 128, qsl],
                                  y32st[:, :], reads=["fa:y32"],
                                  writes=["f:y32h"])
                gst = P.sbuf("f_gst", (128, NB), F32)

                def out_g(dt8, j, pslc, pk):
                    tt(P, gst[:, :], pslc, yb[:, dt8, j:j + NB], ALU.add,
                       reads=[pk, "f:yb"], writes=["f:gst"])
                    act(P, gst[:, :], gst[:, :], AF.Silu,
                        reads=["f:gst"], writes=["f:gst2"])
                    P.dma("scq",
                          env.g_h[dt8 * 128:(dt8 + 1) * 128,
                                  b * L + j:b * L + j + NB],
                          gst[:, :], reads=["f:gst2"], writes=["f:gh"])

                fff("l0f_", yb, "f:yb", out_g, AF.Gelu)



COMMON_INPUTS = [
    ("ones_row", (1, 128), F32), ("ones_col", (128, 1), F32),
    ("ident_b", (128, 128), BF16), ("ident_f", (128, 128), F32),
    ("ones_col_b", (128, 1), BF16),
    ("ident64x2", (128, 64), BF16), ("cvec", (128, 8), F32),
]

L0_INPUTS = [
    ("l0_n1c", (128, 8), F32), ("l0_n2c", (128, 8), F32),
    ("l0_inproj", (D, 4096), F32), ("l0_convw", (128, 16, 4), F32),
    ("l0_convb", (128, 16), F32), ("l0_wdt", (2048, 64), F32),
    ("l0_wB", (2048, 16), F32), ("l0_wC", (2048, 16), F32),
    ("l0_dtw", (64, 2048), F32), ("l0_dtb", (128, 16), F32),
    ("l0_alog", (128, 16, 16), F32), ("l0_D", (128, 16), F32),
    ("l0_outproj", (2048, D), F32), ("l0_w1T", (D, 4096), F32),
    ("l0_w2T", (D, 4096), F32), ("l0_w3T", (4096, D), F32),
    ("l0_sel", (16, 2048), BF16),
    ("l0_expsel", (128, 32, 128), BF16),
] + [item for X in ("q", "k", "v", "f") for item in [
    (f"l0{X}_nwT", (D, 256), F32), (f"l0{X}_nb", (1, 256), F32),
    (f"l0{X}_w1c", (D, 4096), F32), (f"l0{X}_b1c", (128, 32), F32),
    (f"l0{X}_w2c", (4096, D), F32), (f"l0{X}_b2", (256, D), F32),
]]

L1_INPUTS = [
    ("l1_wqkv", (D, 3072), F32), ("l1_bqkv", (1, 3072), F32),
    ("l1_qg", (1, D), F32), ("l1_kg", (1, D), F32),
    ("l1_wout", (D, D), F32), ("l1_bout", (128, 8), F32),
    ("l1_cos", (T, 32), F32), ("l1_sin", (T, 32), F32),
    ("l1_w1T", (D, 4096), F32), ("l1_w2T", (4096, D), F32),
    ("l1_b1c", (128, 32), F32), ("l1_b2c", (128, 8), F32),
]

L2_INPUTS = [
    ("l2_Wq", (D, D), F32), ("l2_Wqr", (D, 512), F32), ("l2_Wkr", (D, 32), F32),
    ("l2_Wkvd", (D, 256), F32), ("l2_Wku", (256, D), F32),
    ("l2_Wvu", (256, D), F32), ("l2_Wo", (D, D), F32),
    ("l2_lobo", (1, 16), F32), ("l2_cos", (T, 16), F32), ("l2_sin", (T, 16), F32),
    ("l2_mask", (128, 128), BF16),
    ("l2_WT", (D, D), F32), ("l2_b", (128, 8), F32),
    ("l2_lu", (50, 4), F32), ("l2_ld", (4, 100), F32),
    ("l2_da", (100, D), F32), ("l2_uaT", (50, D), F32),
]

L3_INPUTS = [
    ("l3_uT", (D, D), F32), ("l3_vT", (D, D), F32),
    ("l3_u", (D, D), F32), ("l3_v", (D, D), F32),
    ("l3_gate", (D, 8), F32), ("l3_w0T", (D, D), F32), ("l3_w1T", (D, D), F32),
    ("l3_b0", (128, 8), F32), ("l3_b1", (128, 8), F32),
]

LAYER_INPUTS = {0: L0_INPUTS, 1: L1_INPUTS, 2: L2_INPUTS,
                3: L3_INPUTS}
LAYER_EMIT = {0: (emit_L0_attn, emit_L0_ffn),
              1: (emit_L1_attn, emit_L1_ffn),
              2: (emit_L2_attn, emit_L2_ffn),
              3: (emit_L3_attn, emit_L3_ffn)}


def build_layer_program(layer, dbg=False):
    nc = bass.Bass("TRN2", target_bir_lowering=False, debug=False)
    env = Env()
    env.hbm = {}

    def inp(name, shape, dtype=F32):
        env.hbm[name] = nc.dram_tensor(name, list(shape), dtype,
                                       kind="ExternalInput").ap()

    inp("xin", (D, T))
    env.x_h = env.hbm["xin"]
    env.xout_h = nc.dram_tensor("xout", [D, T], F32, kind="ExternalOutput").ap()
    env.layer = layer
    kind_dbg = "ExternalOutput" if dbg else "Internal"
    env.dbg = dbg
    env.xn_h = nc.dram_tensor("xn_h", [D, T], BF16, kind=kind_dbg).ap()
    env.s_h = nc.dram_tensor("s_h", [D, T], F32, kind=kind_dbg).ap()
    env.o_h = nc.dram_tensor("o_h", [D, T], BF16, kind=kind_dbg).ap()
    env.pr_dbg = nc.dram_tensor("pr_dbg", [128, 4 * 2048], F32, kind=kind_dbg).ap()
    env.sc_dbg = nc.dram_tensor("sc_dbg", [128, 4 * 2048], F32, kind=kind_dbg).ap()
    env.kr_dbg = nc.dram_tensor("kr_dbg", [64, 2048], BF16, kind=kind_dbg).ap()
    env.krt_dbg = nc.dram_tensor("krt_dbg", [128, 16, 32], F32, kind=kind_dbg).ap()
    env.cbc_dbg = nc.dram_tensor("cbc_dbg", [128, 16], F32, kind=kind_dbg).ap()
    env.den_dbg = nc.dram_tensor("den_dbg", [128, 16], F32, kind=kind_dbg).ap()
    env.nm_dbg = nc.dram_tensor("nm_dbg", [128, 32], F32, kind=kind_dbg).ap()
    env.krr_dbg = nc.dram_tensor("krr_dbg", [128, 16, 32], BF16, kind=kind_dbg).ap()
    env.qr_dbg = nc.dram_tensor("qr_dbg", [64, 8 * 2048], BF16, kind=kind_dbg).ap()
    env.g_h = nc.dram_tensor("g_h", [D, T], F32, kind=kind_dbg).ap()
    if layer == 1:
        env.s1_h = nc.dram_tensor("s1_h", [D, 4096], BF16, kind="Internal").ap()
        env.s2_h = nc.dram_tensor("s2_h", [4096, D], BF16, kind="Internal").ap()
    if layer == 0:
        env.z_h = nc.dram_tensor("z_h", [2048, T], BF16, kind=kind_dbg).ap()
        env.ua_h = nc.dram_tensor("ua_h", [2048, T], BF16, kind=kind_dbg).ap()
        env.yg_h = nc.dram_tensor("yg_h", [2048, T], BF16, kind=kind_dbg).ap()
        env.h_h = nc.dram_tensor("h_h", [D, T], F32, kind=kind_dbg).ap()
        env.xf_h = nc.dram_tensor("xf_h", [D, T], BF16, kind=kind_dbg).ap()
        env.y_h = nc.dram_tensor("y_h", [D, T], BF16, kind="Internal").ap()
        env.hf_h = nc.dram_tensor("hf_h", [4096, L], BF16, kind="Internal").ap()
        env.hf32_h = nc.dram_tensor("hf32_h", [4096, L], F32,
                                    kind="Internal").ap()
        env.xn32_h = nc.dram_tensor("xn32_h", [D, T], F32, kind="Internal").ap()
        env.xf32_h = nc.dram_tensor("xf32_h", [D, L], F32, kind="Internal").ap()
        env.y32_h = nc.dram_tensor("y32_h", [D, L], F32, kind="Internal").ap()
        env.q32_h = nc.dram_tensor("q32_h", [D, L], F32, kind="Internal").ap()
        env.k32_h = nc.dram_tensor("k32_h", [D, L], F32, kind="Internal").ap()
        env.v32_h = nc.dram_tensor("v32_h", [D, L], F32, kind="Internal").ap()
        for X in ("q", "k", "v", "f"):
            for wn, shp in ((f"l0{X}_w1c_b", [D, 4096]),
                            (f"l0{X}_w2c_b", [4096, D])):
                env.hbm[wn] = nc.dram_tensor(wn, shp, BF16, kind="Internal").ap()
        env.xr_dbg = nc.dram_tensor("xr_dbg", [D, T], BF16, kind=kind_dbg).ap()
        env.sx_h = nc.dram_tensor("sx_h", [D, T], F32, kind="Internal").ap()
        env.rs_dbg = nc.dram_tensor("rs_dbg", [128, T], F32, kind=kind_dbg).ap()
        env.scan_dbg = nc.dram_tensor("scan_dbg", [128, 6, L], F32, kind=kind_dbg).ap()
    for nm, shp, dt_ in COMMON_INPUTS + LAYER_INPUTS[layer]:
        inp(nm, shp, dt_)

    import os as _os
    P = Prog(nc)
    emit_consts(P, env)
    emit_layernorm(P, env)
    e_attn, e_ffn = LAYER_EMIT[layer]
    if _os.environ.get("SKIP_ATTN") != "1":
        e_attn(P, env)
    else:
        _z = P.sbuf("z_s", (128, T), F32)
        for i in range(8):
            P.op("vector", lambda e: e.memset(_z[:, :], 0.0), writes=["z:s"])
            P.dma("scq", env.s_h[i * 128:(i + 1) * 128, :], _z[:, :],
                  reads=["z:s"], writes=["z:sh"])
    if _os.environ.get("SKIP_FFN") != "1":
        e_ffn(P, env)
    else:
        _z2 = P.sbuf("z_g", (128, T), F32)
        for i in range(8):
            P.op("vector", lambda e: e.memset(_z2[:, :], 0.0), writes=["z:g"])
            P.dma("scq", env.g_h[i * 128:(i + 1) * 128, :], _z2[:, :],
                  reads=["z:g"], writes=["z:gh"])
    emit_combine(P, env)
    _nd = int(os.environ.get("DUMMY_LOADS", "0"))
    if _nd:
        with P.phase():
            dball = P.sbuf("dummyload", (128, NB), F32)
            _dsrc = os.environ.get("DUMMY_SRC", "x")
            for _i in range(_nd):
                if _dsrc == "h":
                    _ii, _jj = _i % 8, (_i // 8) * NB
                    P.dma("syncq", dball[:, :],
                          env.h_h[_ii * 128:(_ii + 1) * 128, _jj:_jj + NB],
                          reads=["dm:b", "m:hh"], writes=["dm:b"])
                else:
                    P.dma("syncq", dball[:, :], env.x_h[0:128, 0:NB],
                          reads=["dm:b"], writes=["dm:b"])
    if layer == 0 and dbg:
        with P.phase():
            cpt = P.sbuf("hcopy", (128, T), F32)
            for i in range(8):
                P.dma("syncq", cpt[:, :], env.h_h[i * 128:(i + 1) * 128, :],
                      reads=["cp:t", "m:hh"], writes=["cp:t"])
                P.dma("gpq", env.o_h[i * 128:(i + 1) * 128, :], cpt[:, :],
                      reads=["cp:t"], writes=["cp:o"])
    P.finalize()
    return nc


def _shard_x(x):
    out = []
    for c in range(NCORES):
        xc = x[c * B_LOC:(c + 1) * B_LOC].reshape(T, D)
        out.append(np.ascontiguousarray(xc.T.astype(np.float32)))
    return out


def _unshard_x(parts):
    xs = [np.ascontiguousarray(np.asarray(p).T).reshape(B_LOC, L, D)
          for p in parts]
    return np.concatenate(xs, axis=0)


PREP = {0: _prep_L0, 1: _prep_L1, 2: _prep_L2, 3: _prep_L3}


def run_layer(layer, shards, static):
    nc = build_layer_program(layer)
    in_maps = []
    for c in range(NCORES):
        m = dict(static)
        m["xin"] = shards[c]
        in_maps.append(m)
    res = run_bass_kernel_spmd(nc, in_maps, core_ids=list(range(NCORES)))
    return [res.results[c]["xout"] for c in range(NCORES)]


def kernel(**inputs):
    inputs = {k: np.asarray(v) for k, v in inputs.items()}
    consts = _prep_common()
    x = inputs["L0_attn_x"].astype(np.float32)
    shards = _shard_x(x)
    for layer in (0, 1, 2, 3):
        static = dict(consts)
        static.update(PREP[layer](inputs))
        shards = run_layer(layer, shards, static)
    return _unshard_x(shards)
